# version 15
# baseline (speedup 1.0000x reference)
import numpy as np
import ml_dtypes
from contextlib import ExitStack

import concourse.bass as bass
import concourse.mybir as mybir
from concourse.alu_op_type import AluOpType as OP
from concourse.bass_utils import run_bass_kernel_spmd

F32 = mybir.dt.float32
BF16 = mybir.dt.bfloat16
AF = mybir.ActivationFunctionType
AX = mybir.AxisListType.X
BF = ml_dtypes.bfloat16

E = 2048
PROJ = 8512
INTER = 4096
CS = 256
NCONVT = 34
EPS_LN = 1e-5
EPS_RMS = 1e-5
NCORES = 8

C_EPSLN, C_NHALF, C_07, C_1E, C_TINY, C_1I, C_EPSRMS, C_M1, C_ONE = range(9)
C_N = 12


class Ctx:
    def __init__(self, nc):
        self.nc = nc
        self.s = {}
        self.c = {}

    def mk(self, st, names):
        for n in names:
            self.s[n] = st.enter_context(self.nc.semaphore(n))
            self.c[n] = 0

    def I(self, inst, name, n=1):
        inst.then_inc(self.s[name], n)
        self.c[name] += n
        return self.c[name]

    def W(self, eng, name, mark=None):
        v = self.c[name] if mark is None else mark
        if v > 0:
            eng.wait_ge(self.s[name], v)

    def dma(self, q, dst, src, sem="d0"):
        return self.I(q.dma_start(dst, src), sem, 16)

    def fence(self, q, sem="d0"):
        q.wait_ge(self.s[sem], self.c[sem])

    def dmat(self, q, dst, src, sem="d0"):
        return self.I(q.dma_start_transpose(dst, src), sem, 16)


def build_nc(T=2048):
    assert T % 512 == 0
    NCH = T // CS
    TT = T // 128
    TPAD = T + 128
    NXT = TT + 1

    nc = bass.Bass("TRN2", target_bir_lowering=False)
    ctx = Ctx(nc)

    X = nc.dram_tensor("x", [TPAD, E], F32, kind="ExternalInput").ap()
    WIN = nc.dram_tensor("w_in", [PROJ, E], F32, kind="ExternalInput").ap()
    WOUT = nc.dram_tensor("w_out", [E, INTER], F32, kind="ExternalInput").ap()
    FFNW = nc.dram_tensor("ffn_w", [E, E], F32, kind="ExternalInput").ap()
    FFNB = nc.dram_tensor("ffn_b", [1, E], F32, kind="ExternalInput").ap()
    CWFM = nc.dram_tensor("conv_w_fm", [128, NCONVT * 4], F32, kind="ExternalInput").ap()
    CBFM = nc.dram_tensor("conv_b_fm", [128, NCONVT], F32, kind="ExternalInput").ap()
    DTBD = nc.dram_tensor("dtb", [64, 1], F32, kind="ExternalInput").ap()
    ALOGD = nc.dram_tensor("alog", [64, 1], F32, kind="ExternalInput").ap()
    NWREPD = nc.dram_tensor("nwrep", [128, INTER], F32, kind="ExternalInput").ap()
    NW32D = nc.dram_tensor("nw32", [128, 32], F32, kind="ExternalInput").ap()
    DWIDED = nc.dram_tensor("dwide", [128, INTER], F32, kind="ExternalInput").ap()
    L01D = nc.dram_tensor("l01", [128, 128], F32, kind="ExternalInput").ap()
    IDFD = nc.dram_tensor("identf", [128, 128], F32, kind="ExternalInput").ap()
    IDBD = nc.dram_tensor("identb", [128, 128], BF16, kind="ExternalInput").ap()
    ONES1D = nc.dram_tensor("ones1b", [1, 128], BF16, kind="ExternalInput").ap()
    ONES4D = nc.dram_tensor("ones4b", [4, 128], BF16, kind="ExternalInput").ap()

    OUT = nc.dram_tensor("out", [T, E], F32, kind="ExternalOutput").ap()

    xn_scr = nc.dram_tensor("xn_scr", [TPAD, E], BF16, kind="Internal").ap()
    win_scr = nc.dram_tensor("win_scr", [PROJ + 64, E], BF16, kind="Internal").ap()
    wout_scr = nc.dram_tensor("wout_scr", [E, INTER], BF16, kind="Internal").ap()
    q_scr = nc.dram_tensor("q_scr", [E, E], BF16, kind="Internal").ap()
    silg_scr = nc.dram_tensor("silg_scr", [T, INTER], BF16, kind="Internal").ap()
    hid_scr = nc.dram_tensor("hid_scr", [INTER, T], BF16, kind="Internal").ap()
    yfn_scr = nc.dram_tensor("yfn_scr", [T, INTER], BF16, kind="Internal").ap()
    ESP = nc.dram_tensor("esplit_scr", [NCH, 6, 64, CS], BF16, kind="Internal").ap()

    with ExitStack() as S0:
        ctx.mk(S0, ["d0", "d1", "d2", "d3", "d4", "d5", "sv", "ss", "stn", "sg"])
        sb0 = lambda n, sh, dt: S0.enter_context(nc.sbuf_tensor(n, sh, dt))

        CC = sb0("consts", [128, C_N], F32)
        CW = sb0("cwfm_s", [128, NCONVT * 4], F32)
        CB = sb0("cbfm_s", [128, NCONVT], F32)
        DTBS = sb0("dtbs", [64, 1], F32)
        ACOL = sb0("acol", [64, 1], F32)
        L01 = sb0("l01s", [128, 128], F32)
        IDFS = sb0("idfs", [128, 128], F32)
        IDBS = sb0("idbs", [128, 128], BF16)
        ON1B = sb0("on1b", [1, 128], BF16)
        ON4B = sb0("on4b", [4, 128], BF16)
        FBB = sb0("fbb", [1, E], BF16)
        MV = sb0("mv", [128, NXT * 2], F32)
        RSTD = sb0("rstd_ln", [128, NXT], F32)
        NMRS = sb0("nmrs", [128, NXT], F32)
        RSTD2 = sb0("rstd2", [128, TT], F32)
        RSQ = sb0("rsq_all", [128, NCH * 16], F32)

        with nc.named_scope("init"):
            emit_init(ctx, nc, CC, CW, CB, DTBS, ACOL, L01, IDFS, IDBS, ON1B, ON4B,
                      FBB, CWFM, CBFM, DTBD, ALOGD, L01D, IDFD, IDBD, ONES1D,
                      ONES4D, FFNB)
        HS = [(h * 512, 512) for h in range(16)] + [(8192, 384)]

        with ExitStack() as S2:
            sb2 = lambda n, sh, dt: S2.enter_context(nc.sbuf_tensor(n, sh, dt))
            BCB = sb2("bcb", [128, T], BF16)
            BCC = sb2("bcc", [128, T], BF16)
            V1D = sb2("v1d", [128, NCH * 64], F32)
            V2D = sb2("v2d", [128, NCH * 64], F32)
            U0S = sb2("u0s", [128, NCH * 64], F32)
            U1S = sb2("u1s", [128, NCH * 64], F32)
            CREPS = sb2("creps", [128, NCH * 64], F32)
            with ExitStack() as S1:
                DTRAW = S1.enter_context(nc.sbuf_tensor("dtraw", [64, TPAD], F32))
                XNT = S1.enter_context(nc.sbuf_tensor("xnt", [128, 16 * (T + 16)],
                                                      BF16))
                with nc.named_scope("p1a"):
                    wcast_m, xnt_ch = emit_p1a(ctx, nc, T, NXT, X, xn_scr, MV,
                                               RSTD, NMRS, CC, WIN, win_scr,
                                               XNT, HS)
                with nc.named_scope("p1b"):
                    emit_p1b(ctx, nc, T, TT, TPAD, xn_scr, win_scr, silg_scr,
                             hid_scr, DTRAW, CW, CB, BCB, BCC, CC, WOUT,
                             wout_scr, wcast_m, xnt_ch, XNT, HS, WIN, FFNW,
                             q_scr)
                with nc.named_scope("p15"):
                    emit_p15(ctx, nc, T, NCH, DTRAW, DTBS, ACOL, CC, IDFS, IDBS,
                             ON4B, ESP, V1D, V2D, U0S, U1S, CREPS)

            with nc.named_scope("p2"):
                emit_p2(ctx, nc, T, NCH, hid_scr, silg_scr, yfn_scr, ESP, RSQ,
                        DWIDED, BCB, BCC, L01, CREPS, V1D, V2D, U0S, U1S, CC)

        with nc.named_scope("p3"):
            emit_p3(ctx, nc, T, TT, X, xn_scr, wout_scr, yfn_scr, q_scr, OUT,
                    RSTD2, NW32D, RSQ, CC, ON1B, FBB)

        with nc.Block() as blk:
            def fin(q):
                ctx.W(q, "d0")
                ctx.W(q, "d1")
                ctx.W(q, "d2")
                ctx.W(q, "d3")
                ctx.W(q, "d4")
                ctx.W(q, "d5")
            blk.sync(fin)
    return nc


def emit_init(ctx, nc, CC, CW, CB, DTBS, ACOL, L01, IDFS, IDBS, ON1B, ON4B,
              FBB, CWFM, CBFM, DTBD, ALOGD, L01D, IDFD, IDBD, ONES1D, ONES4D,
              FFNB):
    with nc.Block() as blk:
        def f_loads(q):
            for dst, src in [(CW[:], CWFM), (CB[:], CBFM), (DTBS[:], DTBD),
                             (L01[:], L01D), (IDFS[:], IDFD), (IDBS[:], IDBD),
                             (ON1B[:], ONES1D), (ON4B[:], ONES4D),
                             (ACOL[:], ALOGD)]:
                ctx.dma(q, dst, src)
            ctx.fence(q)
        blk.sync(f_loads)

        def f_cast(g):
            ctx.dma(g, FBB[:], FFNB, "d1")
            ctx.fence(g, "d1")
        blk.gpsimd(f_cast)

        def f_sets(v):
            for i, val in enumerate([EPS_LN, -0.5, 0.7, 1.0 / E, 1e-18,
                                     1.0 / INTER, EPS_RMS, -1.0, 1.0]):
                ctx.I(v.memset(CC[:, i:i + 1], val), "sv")
        blk.vector(f_sets)

        def f_exp(s):
            ctx.W(s, "d0")
            ctx.I(s.activation(ACOL[:], ACOL[:], AF.Exp), "ss")
        blk.scalar(f_exp)

        def f_neg(v):
            ctx.W(v, "ss")
            ctx.I(v.tensor_scalar_mul(ACOL[:], ACOL[:], -1.0), "sv")
        blk.vector(f_neg)

        def f_dr(q):
            ctx.W(q, "d1")
        blk.sync(f_dr)


def emit_p1a(ctx, nc, T, NXT, X, xn_scr, MV, RSTD, NMRS, CC,
             WIN, win_scr, XNT, HS):
    H1 = (NXT + 1) // 2
    XW = T + 16
    with nc.Block() as blk, ExitStack() as ps:
        sb = lambda n, sh, dt: ps.enter_context(nc.sbuf_tensor(n, sh, dt))
        XBIG = sb("p1xbig", [128, H1 * E], F32)
        BNS = sb("p1bns", [128, 24], F32)
        XNB = [sb(f"p1xnb{i}", [128, E], BF16) for i in range(2)]


        ZPAD = sb("zpad", [64, E], BF16)
        zpad_m = [0]

        def f_zpad(v):
            zpad_m[0] = ctx.I(v.memset(ZPAD[:], 0.0), "sv")
        blk.vector(f_zpad)

        wcast_m = []

        def f_wprep(g):
            ctx.W(g, "sv", zpad_m[0])
            ctx.dma(g, win_scr[PROJ:PROJ + 64, :], ZPAD[:], "d1")
            ctx.fence(g, "d1")
            for jlo, wd in HS[:6]:
                m = ctx.dma(g, win_scr[jlo:jlo + wd, :],
                            WIN[jlo:jlo + wd, :], "d1")
                ctx.fence(g, "d1")
                wcast_m.append(m)
        blk.gpsimd(f_wprep)

        xnt_ch = [0, 0]
        xl_m = [0]
        xnact_m = [0, 0]
        xnst_m = [0, 0]
        for half in range(2):
            lo = half * H1
            hi = min(NXT, lo + H1)
            n = hi - lo

            def f_xl(q, lo=lo, n=n, half=half):
                if half:
                    ctx.W(q, "ss", max(xnact_m))
                xl_m[0] = ctx.dma(
                    q, XBIG[:, 0:n * E].rearrange("p (a e) -> p a e", e=E),
                    X[lo * 128:hi * 128, :].rearrange("(a p) e -> p a e", p=128))
                ctx.fence(q)
            blk.sync(f_xl)

            def f_bn(v, lo=lo, n=n):
                ctx.W(v, "d0", xl_m[0])
                ctx.W(v, "sv")
                for a in range(n):
                    m = 0
                    for k in range(4):
                        m = ctx.I(v.bn_stats(
                            BNS[:, k * 6:(k + 1) * 6],
                            XBIG[:, a * E + k * 512: a * E + (k + 1) * 512]), "sv")
                    ctx.W(v, "sv", m)
                    m = ctx.I(v.bn_aggr(MV[:, (lo + a) * 2:(lo + a) * 2 + 2],
                                        BNS[:]), "sv")
                    ctx.W(v, "sv", m)
            blk.vector(f_bn)

            def f_rs(s, lo=lo, hi=hi):
                ctx.W(s, "sv")
                m = ctx.I(s.activation(RSTD[:, lo:hi], MV[:, lo * 2 + 1:hi * 2:2],
                                       AF.Ln, bias=CC[:, C_EPSLN:C_EPSLN + 1],
                                       scale=1.0), "ss")
                ctx.W(s, "ss", m)
                ctx.I(s.activation(RSTD[:, lo:hi], RSTD[:, lo:hi], AF.Exp,
                                   scale=CC[:, C_NHALF:C_NHALF + 1]), "ss")
            blk.scalar(f_rs)

            nm_done = [0]

            def f_nm(v, lo=lo, hi=hi):
                ctx.W(v, "ss")
                m = ctx.I(v.tensor_tensor(NMRS[:, lo:hi], MV[:, lo * 2:hi * 2:2],
                                          RSTD[:, lo:hi], OP.mult), "sv")
                ctx.W(v, "sv", m)
                nm_done[0] = ctx.I(v.tensor_scalar_mul(NMRS[:, lo:hi],
                                                       NMRS[:, lo:hi], -1.0), "sv")
            blk.vector(f_nm)

            for a in range(n):
                xt = lo + a
                b = xt % 2

                def f_xn(s, b=b, xt=xt, a=a):
                    ctx.W(s, "sv", nm_done[0])
                    if xnst_m[b]:
                        ctx.W(s, "d0", xnst_m[b])
                    xnact_m[b] = ctx.I(s.activation(XNB[b][:],
                                                    XBIG[:, a * E:(a + 1) * E],
                                                    AF.Identity,
                                                    bias=NMRS[:, xt:xt + 1],
                                                    scale=RSTD[:, xt:xt + 1]), "ss")
                blk.scalar(f_xn)

                def f_xs(q, b=b, xt=xt):
                    ctx.W(q, "ss", xnact_m[b])
                    xnst_m[b] = ctx.dma(q, xn_scr[xt * 128:(xt + 1) * 128, :],
                                        XNB[b][:])
                    ctx.fence(q)
                blk.sync(f_xs)

            r0, r1 = (0, H1 * 128) if half == 0 else (H1 * 128, XW)
            r1 = min(r1, XW)

            def f_xnt(q, r0=r0, r1=r1, half=half):
                xnt_ch[half] = ctx.dmat(
                    q, XNT[:].rearrange("p (m k) -> p m k", k=XW)[:, :, r0:r1],
                    xn_scr[r0:r1, :], sem="d5" if half == 0 else "d0")
                if half == 1:
                    ctx.fence(q)
            blk.sync(f_xnt)

        def f_dr(q):
            ctx.W(q, "d0")
        blk.sync(f_dr)
        return wcast_m, xnt_ch


def emit_p1b(ctx, nc, T, TT, TPAD, xn_scr, win_scr, silg_scr, hid_scr,
             DTRAW, CW, CB, BCB, BCC, CC, WOUT, wout_scr, wcast_m, xnt_ch,
             XNT, HS, WIN, FFNW, q_scr):
    with nc.Block() as blk, ExitStack() as ps:
        sb = lambda n, sh, dt: ps.enter_context(nc.sbuf_tensor(n, sh, dt))
        FW = [sb(f"p0fw{i}", [128, E], F32) for i in range(1)]
        QA = sb("p0qa", [128, E], BF16)
        QBT = [sb(f"p0qb{i}", [128, E], BF16) for i in range(1)]
        TH = sb("p0th", [128, 4], F32)
        q_fl = [0, 0]
        q_dv = [0, 0]
        q_st = [0, 0]

        def f_wcast_rest(g):
            for jlo, wd in HS[6:]:
                wd = min(wd, PROJ - jlo)
                m = ctx.dma(g, win_scr[jlo:jlo + wd, :],
                            WIN[jlo:jlo + wd, :], "d1")
                ctx.fence(g, "d1")
                wcast_m.append(m)
        blk.gpsimd(f_wcast_rest)

        def emit_quant(ot):
            b = 0

            def f_fl(g, ot=ot, b=b):
                if q_dv[b]:
                    ctx.W(g, "sv", q_dv[b])
                q_fl[b] = ctx.dma(g, FW[b][:], FFNW[ot * 128:(ot + 1) * 128, :],
                                  "d2")
                ctx.fence(g, "d2")
            blk.gpsimd(f_fl)

            def f_qv(v, ot=ot, b=b):
                ctx.W(v, "d2", q_fl[b])
                ctx.W(v, "sv")
                m = ctx.I(v.tensor_reduce(TH[:, 0:1], FW[b][:], AX, OP.add,
                                          apply_absolute_value=True), "sv")
                ctx.W(v, "sv", m)
                m = ctx.I(v.tensor_scalar(TH[:, 1:2], TH[:, 0:1],
                                          CC[:, C_1E:C_1E + 1],
                                          CC[:, C_07:C_07 + 1], OP.mult,
                                          OP.mult), "sv")
                ctx.W(v, "sv", m)
                ctx.I(v.tensor_scalar(QA[:], FW[b][:], TH[:, 1:2], None,
                                      OP.is_gt), "sv")
                m = ctx.I(v.tensor_scalar(FW[b][:], FW[b][:], -1.0, TH[:, 1:2],
                                          OP.mult, OP.is_gt), "sv")
                ctx.W(v, "sv", m)
                q_dv[b] = ctx.I(v.tensor_tensor(QBT[b][:], QA[:], FW[b][:],
                                                OP.subtract), "sv")
            blk.vector(f_qv)

            def f_qs(g, ot=ot, b=b):
                ctx.W(g, "sv", q_dv[b])
                ctx.dma(g, q_scr[ot * 128:(ot + 1) * 128, :], QBT[b][:], "d2")
                ctx.fence(g, "d2")
            blk.gpsimd(f_qs)

        def f_woprep(g):
            ctx.dma(g, wout_scr, WOUT, "d2")
            ctx.fence(g, "d2")
        blk.gpsimd(f_woprep)

        XW = T + 16
        WTS = sb("wts", [128, 16 * 1024], BF16)
        SILGB = [sb(f"silgb{i}", [128, 512], BF16) for i in range(2)]
        CONVIN = [sb(f"convin{i}", [128, TPAD], F32) for i in range(2)]
        CVA = [sb(f"cva{i}", [128, T], F32) for i in range(2)]
        CVB = [sb(f"cvb{i}", [128, T], F32) for i in range(2)]
        HOUT = [sb(f"hout{i}", [128, T], BF16) for i in range(2)]
        PSG = [ps.enter_context(nc.psum_tensor(f"psg{i}", [128, 512], F32))
               for i in range(8)]

        def xnt_ap(it, lo, hi):
            return XNT[:, it * XW + lo: it * XW + hi]

        wts_m = [0, 0]
        wts_cons = [0, 0]

        def load_half(h):
            jlo, wd = HS[h]
            r = h % 2

            def f_wl(q, jlo=jlo, wd=wd, r=r, h=h):
                ctx.W(q, "d1", wcast_m[h])
                if wts_cons[r]:
                    ctx.W(q, "stn", wts_cons[r])
                dst = WTS[:].rearrange("p (m k) -> p m k", k=1024)[:, :, r * 512:r * 512 + wd]
                wts_m[r] = ctx.dmat(q, dst, win_scr[jlo:jlo + wd, :])
                ctx.fence(q)
            blk.sync(f_wl)

        load_half(0)

        psg_ev = [0] * 8
        silg_st = [0, 0]
        for h in range(8):
            r = h % 2
            load_half(h + 1)
            emit_quant(h)
            for tt0 in range(0, TT, 2):
                gmm = {}
                for tt in (tt0, tt0 + 1):
                    pb = (h * TT + tt) % 8

                    def f_mm(t, tt=tt, r=r, pb=pb):
                        ctx.W(t, "d0", wts_m[r])
                        if tt < 8:
                            ctx.W(t, "d5", xnt_ch[0])
                        else:
                            ctx.W(t, "d0", xnt_ch[1])
                        if psg_ev[pb]:
                            ctx.W(t, "ss", psg_ev[pb])
                        for it in range(16):
                            wts_cons[r] = ctx.I(t.matmul(
                                PSG[pb][:],
                                xnt_ap(it, 3 + tt * 128, 131 + tt * 128),
                                WTS[:, it * 1024 + r * 512: it * 1024 + r * 512 + 512],
                                start=(it == 0), stop=(it == 15)), "stn")
                        gmm[tt] = wts_cons[r]
                    blk.tensor(f_mm)

                def f_ev(s, tt0=tt0, h=h, r=r, gmm=gmm):
                    for tt in (tt0, tt0 + 1):
                        ctx.W(s, "stn", gmm[tt])
                        pb = (h * TT + tt) % 8
                        sbuf_b = tt % 2
                        if silg_st[sbuf_b]:
                            ctx.W(s, "d0", silg_st[sbuf_b])
                        psg_ev[pb] = ctx.I(s.activation(
                            SILGB[sbuf_b][:], PSG[pb][:], AF.Silu), "ss")
                blk.scalar(f_ev)

                def f_sg(q, tt0=tt0, h=h):
                    for tt in (tt0, tt0 + 1):
                        pb = (h * TT + tt) % 8
                        sbuf_b = tt % 2
                        ctx.W(q, "ss", psg_ev[pb])
                        ctx.dma(
                            q, silg_scr[tt * 128:(tt + 1) * 128,
                                        h * 512:(h + 1) * 512],
                            SILGB[sbuf_b][:])
                    ctx.fence(q)
                    silg_st[0] = silg_st[1] = ctx.c["d0"]
                blk.sync(f_sg)

        TW = []
        off = 0
        while off < XW:
            TW.append((off, min(512, XW - off)))
            off += 512
        hst_m = [0, 0]
        conv_m = [0, 0]
        cin_cons = [0, 0]
        si_pend = [None]
        for h in range(8, 17):
            r = h % 2
            if h + 1 < 17:
                load_half(h + 1)
            if h - 8 < 8:
                emit_quant(8 + (h - 8))
            jlo, wd = HS[h]
            for jt in range(wd // 128):
                jglob = (jlo - 4096) // 128 + jt
                cb = jglob % 2
                ev_marks = [0]
                wmm = {}

                def emit_mmw(wi, lo, wdw, jt=jt, r=r):
                    def f_mmw(t, wi=wi, lo=lo, wdw=wdw, jt=jt, r=r):
                        ctx.W(t, "d0", wts_m[r])
                        ctx.W(t, "d0", xnt_ch[1])
                        if psg_ev[wi % 8]:
                            ctx.W(t, "ss", psg_ev[wi % 8])
                        for it in range(16):
                            wts_cons[r] = ctx.I(t.matmul(
                                PSG[wi % 8][:, 0:wdw],
                                WTS[:, it * 1024 + r * 512 + jt * 128:
                                    it * 1024 + r * 512 + jt * 128 + 128],
                                xnt_ap(it, lo, lo + wdw),
                                start=(it == 0), stop=(it == 15)), "stn")
                        wmm[wi] = wts_cons[r]
                    blk.tensor(f_mmw)

                def emit_evw(wis, with_si, jglob=jglob, cb=cb, wmm=wmm):
                    def f_evw(s, wis=wis, with_si=with_si, jglob=jglob, cb=cb):
                        for wi in wis:
                            ctx.W(s, "stn", wmm[wi])
                            lo, wdw = TW[wi]
                            if wi == 0 and cin_cons[cb]:
                                ctx.W(s, "sv", cin_cons[cb])
                            if jglob < NCONVT:
                                psg_ev[wi % 8] = ctx.I(
                                    s.copy(CONVIN[cb][:, lo:lo + wdw],
                                           PSG[wi % 8][:, 0:wdw]), "ss")
                            else:
                                psg_ev[wi % 8] = ctx.I(
                                    s.copy(DTRAW[:, lo:lo + wdw],
                                           PSG[wi % 8][0:64, 0:wdw]), "ss")
                            ev_marks[0] = psg_ev[wi % 8]
                        if with_si and si_pend[0] is not None:
                            pj, pcb = si_pend[0]
                            ctx.W(s, "sv", cin_cons[pcb])
                            if pj < 32:
                                dst = HOUT[pcb][:]
                                if hst_m[pcb]:
                                    ctx.W(s, "d0", hst_m[pcb])
                            elif pj == 32:
                                dst = BCB[:]
                            else:
                                dst = BCC[:]
                            conv_m[pcb] = ctx.I(s.activation(
                                dst, CVB[pcb][:], AF.Silu,
                                bias=CB[:, pj:pj + 1], scale=1.0), "ss")
                            si_pend[0] = None
                    blk.scalar(f_evw)

                nw = len(TW)
                for wi in range(min(4, nw)):
                    emit_mmw(wi, TW[wi][0], TW[wi][1])
                emit_evw(list(range(min(4, nw))), with_si=True)
                if nw > 4:
                    for wi in range(4, nw):
                        emit_mmw(wi, TW[wi][0], TW[wi][1])
                    emit_evw(list(range(4, nw)), with_si=False)

                if jglob < NCONVT:
                    def f_cv(v, jglob=jglob, cb=cb):
                        ctx.W(v, "ss", ev_marks[0])
                        if conv_m[cb]:
                            ctx.W(v, "ss", conv_m[cb])
                        ctx.W(v, "sv")
                        m = ctx.I(v.tensor_scalar(
                            CVA[cb][:], CONVIN[cb][:, 0:T],
                            CW[:, jglob * 4:jglob * 4 + 1], None, OP.mult), "sv")
                        ctx.W(v, "sv", m)
                        m = ctx.I(v.scalar_tensor_tensor(
                            CVB[cb][:], CONVIN[cb][:, 1:1 + T],
                            CW[:, jglob * 4 + 1:jglob * 4 + 2], CVA[cb][:],
                            OP.mult, OP.add), "sv")
                        ctx.W(v, "sv", m)
                        m = ctx.I(v.scalar_tensor_tensor(
                            CVA[cb][:], CONVIN[cb][:, 2:2 + T],
                            CW[:, jglob * 4 + 2:jglob * 4 + 3], CVB[cb][:],
                            OP.mult, OP.add), "sv")
                        ctx.W(v, "sv", m)
                        cin_cons[cb] = ctx.I(v.scalar_tensor_tensor(
                            CVB[cb][:], CONVIN[cb][:, 3:3 + T],
                            CW[:, jglob * 4 + 3:jglob * 4 + 4], CVA[cb][:],
                            OP.mult, OP.add), "sv")
                    blk.vector(f_cv)
                    si_pend[0] = (jglob, cb)

                    if jglob >= 1 and (jglob - 1) < 32:
                        def f_hs(q, pj=jglob - 1):
                            pcb = pj % 2
                            ctx.W(q, "ss", conv_m[pcb])
                            hst_m[pcb] = ctx.dma(
                                q, hid_scr[pj * 128:(pj + 1) * 128, :],
                                HOUT[pcb][:])
                            ctx.fence(q)
                        blk.sync(f_hs)

        def f_sil(s):
            if si_pend[0] is not None:
                pj, pcb = si_pend[0]
                ctx.W(s, "sv", cin_cons[pcb])
                dst = BCC[:] if pj == 33 else (BCB[:] if pj == 32 else HOUT[pcb][:])
                conv_m[pcb] = ctx.I(s.activation(
                    dst, CVB[pcb][:], AF.Silu,
                    bias=CB[:, pj:pj + 1], scale=1.0), "ss")
                si_pend[0] = None
        blk.scalar(f_sil)

        def f_dr(q):
            ctx.W(q, "d0")
            ctx.W(q, "d1")
        blk.sync(f_dr)


def emit_p15(ctx, nc, T, NCH, DTRAW, DTBS, ACOL, CC, IDFS, IDBS, ON4B,
             ESP, V1D, V2D, U0S, U1S, CREPS):
    with nc.Block() as blk, ExitStack() as ps:
        sb = lambda n, sh, dt: ps.enter_context(nc.sbuf_tensor(n, sh, dt))
        SP1 = sb("sp1", [64, T], F32)
        MX = sb("mx_t", [64, T], F32)
        DT = sb("dt_t", [64, T], F32)
        ACUM = sb("acum", [64, T], F32)
        LNDT = sb("lndt", [64, T], F32)
        NN = sb("nn_t", [64, T], F32)
        NEG = sb("neg_t", [64, T], F32)
        SC2 = sb("sc2", [64, CS], F32)
        SPL = sb("spl", [64, T], BF16)
        RF = sb("rf_t", [64, T], F32)
        RES = sb("res_t", [64, T], F32)
        CHS = sb("chs", [64, NCH * 8], F32)
        VTMP = sb("vtmp", [64, NCH * 512], F32)
        CST = sb("cst", [64, NCH * 4], BF16)
        CR4 = [sb(f"cr4_{i}", [4, 64], BF16) for i in range(2)]
        PTF = [ps.enter_context(nc.psum_tensor(f"ptf{i}", [128, 512], F32))
               for i in range(2)]
        PTB = [ps.enter_context(nc.psum_tensor(f"ptb{i}", [4, 64], BF16))
               for i in range(2)]

        def f_v1(v):
            ctx.W(v, "ss")
            ctx.I(v.tensor_scalar(MX[:], DTRAW[:, 3:3 + T], DTBS[:], 0.0,
                                  OP.add, OP.max), "sv")
        blk.vector(f_v1)

        def f_s1(s):
            ctx.W(s, "ss")
            m = ctx.I(s.activation(SP1[:], DTRAW[:, 3:3 + T], AF.Abs,
                                   bias=DTBS[:], scale=1.0), "ss")
            ctx.W(s, "ss", m)
            m = ctx.I(s.activation(SP1[:], SP1[:], AF.Exp,
                                   scale=CC[0:64, C_M1:C_M1 + 1]), "ss")
            ctx.W(s, "ss", m)
            ctx.I(s.activation(SP1[:], SP1[:], AF.Ln, bias=1.0), "ss")
        blk.scalar(f_s1)

        def f_v2(v):
            ctx.W(v, "ss")
            ctx.W(v, "sv")
            m = ctx.I(v.tensor_tensor(DT[:], MX[:], SP1[:], OP.add), "sv")
            ctx.W(v, "sv", m)
            m = ctx.I(v.tensor_scalar(MX[:], DT[:], ACOL[:], None, OP.mult), "sv")
            ctx.W(v, "sv", m)
            ctx.I(v.tensor_scalar(SP1[:], DT[:], CC[0:64, C_TINY:C_TINY + 1],
                                  None, OP.max), "sv")
        blk.vector(f_v2)

        def f_s2(s):
            ctx.W(s, "sv")
            ctx.I(s.activation(LNDT[:], SP1[:], AF.Ln), "ss")
        blk.scalar(f_s2)

        SC2A = sb("sc2a", [64, T], F32)

        def f_cum(v):
            ctx.W(v, "sv")
            curs = [MX[:, c * CS:(c + 1) * CS] for c in range(NCH)]
            for step in range(8):
                k = 1 << step
                m = 0
                for c in range(NCH):
                    lo = c * CS
                    dst = (SC2A[:, lo:lo + CS] if step % 2 == 0
                           else ACUM[:, lo:lo + CS])
                    cur = curs[c]
                    ctx.I(v.tensor_copy(dst[:, 0:k], cur[:, 0:k]), "sv")
                    m = ctx.I(v.tensor_tensor(dst[:, k:CS], cur[:, k:CS],
                                              cur[:, 0:CS - k], OP.add), "sv")
                    curs[c] = dst
                ctx.W(v, "sv", m)
        blk.vector(f_cum)

        def f_v3(v):
            ctx.W(v, "sv")
            ctx.W(v, "ss")
            m = ctx.I(v.scalar_tensor_tensor(NN[:], ACUM[:], -1.0, LNDT[:],
                                             OP.mult, OP.add), "sv")
            ctx.W(v, "sv", m)
            for c in range(NCH):
                lo = c * CS
                ctx.I(v.tensor_scalar(NEG[:, lo:lo + CS], NN[:, lo:lo + CS],
                                      ACUM[:, lo + CS - 1:lo + CS], None,
                                      OP.add), "sv")
        blk.vector(f_v3)

        def emit_split(src, base_row):
            spl_m = [0]

            def f_sa(v, src=src):
                ctx.W(v, "sv")
                ctx.W(v, "d0")
                spl_m[0] = ctx.I(v.tensor_copy(SPL[:], src[:]), "sv")
            blk.vector(f_sa)

            def f_wa(q, base_row=base_row):
                ctx.W(q, "sv", spl_m[0])
                for c in range(NCH):
                    ctx.dma(q, ESP[c:c + 1, base_row:base_row + 1, :, :].squeeze(0).squeeze(0),
                            SPL[:, c * CS:(c + 1) * CS])
                ctx.fence(q)
            blk.sync(f_wa)

            def f_sbf(v, src=src):
                ctx.W(v, "sv")
                m = ctx.I(v.tensor_copy(RF[:], SPL[:]), "sv")
                ctx.W(v, "sv", m)
                m = ctx.I(v.tensor_tensor(RES[:], src[:], RF[:], OP.subtract), "sv")
                ctx.W(v, "sv", m)
                ctx.W(v, "d0")
                spl_m[0] = ctx.I(v.tensor_copy(SPL[:], RES[:]), "sv")
            blk.vector(f_sbf)

            def f_wb(q, base_row=base_row):
                ctx.W(q, "sv", spl_m[0])
                for c in range(NCH):
                    ctx.dma(q, ESP[c:c + 1, base_row + 1:base_row + 2, :, :].squeeze(0).squeeze(0),
                            SPL[:, c * CS:(c + 1) * CS])
                ctx.fence(q)
            blk.sync(f_wb)

            def f_sc(v):
                ctx.W(v, "sv")
                m = ctx.I(v.tensor_copy(RF[:], SPL[:]), "sv")
                ctx.W(v, "sv", m)
                m = ctx.I(v.tensor_tensor(RES[:], RES[:], RF[:], OP.subtract), "sv")
                ctx.W(v, "sv", m)
                ctx.W(v, "d0")
                spl_m[0] = ctx.I(v.tensor_copy(SPL[:], RES[:]), "sv")
            blk.vector(f_sc)

            def f_wc(q, base_row=base_row):
                ctx.W(q, "sv", spl_m[0])
                for c in range(NCH):
                    ctx.dma(q, ESP[c:c + 1, base_row + 2:base_row + 3, :, :].squeeze(0).squeeze(0),
                            SPL[:, c * CS:(c + 1) * CS])
                ctx.fence(q)
            blk.sync(f_wc)

        emit_split(ACUM, 0)
        emit_split(NEG, 3)

        def f_cs(s):
            ctx.W(s, "sv")
            m = 0
            for c in range(NCH):
                lo = c * CS
                m = ctx.I(s.activation(CHS[:, c * 8:c * 8 + 1],
                                       ACUM[:, lo + CS - 1:lo + CS], AF.Exp), "ss")
            ctx.W(s, "ss", m)
            for c in range(NCH):
                m = ctx.I(s.activation(CHS[:, c * 8 + 1:c * 8 + 2],
                                       CHS[:, c * 8:c * 8 + 1], AF.Ln,
                                       bias=1.0), "ss")
        blk.scalar(f_cs)

        def f_cv2(v):
            ctx.W(v, "ss")
            ctx.W(v, "sv")
            for c in range(NCH):
                lo = c * CS
                o8 = c * 8
                A255 = ACUM[:, lo + CS - 1:lo + CS]
                P127 = ACUM[:, lo + 127:lo + 128]
                Q0 = ACUM[:, lo:lo + 1]
                sp = CHS[:, o8 + 1:o8 + 2]
                ctx.I(v.tensor_tensor(CHS[:, o8 + 2:o8 + 3], sp, A255,
                                      OP.subtract), "sv")
                ctx.I(v.tensor_tensor(CHS[:, o8 + 3:o8 + 4], sp, P127,
                                      OP.subtract), "sv")
                ctx.I(v.tensor_tensor(CHS[:, o8 + 4:o8 + 5], Q0, A255,
                                      OP.add), "sv")
                ctx.I(v.tensor_scalar_mul(CHS[:, o8 + 5:o8 + 6], Q0, -1.0), "sv")
        blk.vector(f_cv2)

        def f_uv(s):
            ctx.W(s, "sv")
            ctx.W(s, "ss")
            for c in range(NCH):
                lo = c * CS
                o8 = c * 8
                ctx.I(s.activation(VTMP[:, c * 512:c * 512 + 128],
                                   NN[:, lo:lo + 128], AF.Exp,
                                   bias=ACUM[:, lo + 127:lo + 128], scale=1.0), "ss")
                ctx.I(s.activation(VTMP[:, c * 512 + 128:c * 512 + 256],
                                   NN[:, lo + 128:lo + 256], AF.Exp,
                                   bias=CHS[:, o8 + 4:o8 + 5], scale=1.0), "ss")
                ctx.I(s.activation(VTMP[:, c * 512 + 256:c * 512 + 384],
                                   ACUM[:, lo + 128:lo + 256], AF.Exp,
                                   bias=CHS[:, o8 + 3:o8 + 4], scale=1.0), "ss")
                ctx.I(s.activation(VTMP[:, c * 512 + 384:c * 512 + 512],
                                   ACUM[:, lo:lo + 128], AF.Exp,
                                   bias=CHS[:, o8 + 5:o8 + 6], scale=1.0), "ss")
        blk.scalar(f_uv)

        def f_cspl(v):
            ctx.W(v, "sv")
            for lvl in range(4):
                m = 0
                for c in range(NCH):
                    o8 = c * 8
                    src = CHS[:, o8 + 2:o8 + 3] if lvl == 0 else CHS[:, o8 + 6:o8 + 7]
                    m = ctx.I(v.tensor_copy(CST[:, c * 4 + lvl:c * 4 + lvl + 1],
                                            src), "sv")
                ctx.W(v, "sv", m)
                if lvl < 3:
                    for c in range(NCH):
                        o8 = c * 8
                        m = ctx.I(v.tensor_copy(
                            CHS[:, o8 + 7:o8 + 8],
                            CST[:, c * 4 + lvl:c * 4 + lvl + 1]), "sv")
                    ctx.W(v, "sv", m)
                    for c in range(NCH):
                        o8 = c * 8
                        src = CHS[:, o8 + 2:o8 + 3] if lvl == 0 else CHS[:, o8 + 6:o8 + 7]
                        m = ctx.I(v.tensor_tensor(CHS[:, o8 + 6:o8 + 7], src,
                                                  CHS[:, o8 + 7:o8 + 8],
                                                  OP.subtract), "sv")
                    ctx.W(v, "sv", m)
        blk.vector(f_cspl)

        tp_m = [0, 0]
        tev_m = [0, 0]
        ctp_m = [0, 0]
        crv_m = [0, 0]
        crm_m = [0, 0]
        crev_m = [0, 0]
        for c in range(NCH):
            pp = c % 2

            def f_tp(t, c=c, pp=pp):
                if c == 0:
                    ctx.W(t, "ss")
                if tev_m[pp]:
                    ctx.W(t, "sv", tev_m[pp])
                m = 0
                for k in range(4):
                    m = ctx.I(t.transpose(PTF[pp][:, k * 64:(k + 1) * 64],
                                          VTMP[:, c * 512 + k * 128:
                                               c * 512 + (k + 1) * 128],
                                          IDFS[0:64, 0:64]), "stn")
                tp_m[pp] = m
                if c == 0:
                    ctx.W(t, "sv")
                if crv_m[pp]:
                    ctx.W(t, "sv", crv_m[pp])
                ctp_m[pp] = ctx.I(t.transpose(
                    PTB[pp][0:4, 0:64], CST[:, c * 4:(c + 1) * 4],
                    IDBS[0:64, 0:64]), "stn")
            blk.tensor(f_tp)

            def f_tev(v, c=c, pp=pp):
                ctx.W(v, "stn", tp_m[pp])
                m = 0
                for k, dstt in enumerate([V1D, V2D, U1S, U0S]):
                    m = ctx.I(v.tensor_copy(dstt[:, c * 64:(c + 1) * 64],
                                            PTF[pp][:, k * 64:(k + 1) * 64]), "sv")
                tev_m[pp] = m
                ctx.W(v, "stn", ctp_m[pp])
                crv_m[pp] = ctx.I(v.tensor_copy(CR4[pp][:], PTB[pp][0:4, 0:64]),
                                  "sv")
            blk.vector(f_tev)

            def f_crm(t, c=c, pp=pp):
                ctx.W(t, "sv", crv_m[pp])
                if crev_m[pp]:
                    ctx.W(t, "sv", crev_m[pp])
                crm_m[pp] = ctx.I(t.matmul(PTF[pp][:, 256:320], ON4B[:],
                                           CR4[pp][:], start=True, stop=True),
                                  "stn")
            blk.tensor(f_crm)

            def f_crev(v, c=c, pp=pp):
                ctx.W(v, "stn", crm_m[pp])
                crev_m[pp] = ctx.I(v.tensor_copy(
                    CREPS[:, c * 64:(c + 1) * 64], PTF[pp][:, 256:320]), "sv")
            blk.vector(f_crev)

        def f_dr(q):
            ctx.W(q, "d0")
        blk.sync(f_dr)


def emit_p2(ctx, nc, T, NCH, hid_scr, silg_scr, yfn_scr, ESP, RSQ, DWIDED,
            BCB, BCC, L01, CREPS, V1D, V2D, U0S, U1S, CC):
    with nc.Block() as blk, ExitStack() as ps:
        sb = lambda n, sh, dt: ps.enter_context(nc.sbuf_tensor(n, sh, dt))
        HS2 = [[sb(f"hs2_{lt}{p}", [128, INTER], BF16) for p in range(2)]
               for lt in range(2)]
        SILGT = [[sb(f"silgt{lt}{p}", [128, INTER], BF16) for p in range(2)]
                 for lt in range(2)]
        DWIDES = sb("dwides", [128, INTER], BF16)
        GMSB = [sb(f"gmsb{p}", [128, 512], BF16) for p in range(2)]
        LHSB = [sb(f"lhsb{i}", [6, 2048], BF16) for i in range(2)]
        RHSB = [sb(f"rhsb{i}", [6, 2048], BF16) for i in range(2)]
        L01C = [sb(f"l01c{i}", [128, 1024], F32) for i in range(2)]
        TWB = [sb(f"twb{i}", [128, 1024], F32) for i in range(2)]
        EXW = [sb(f"exw{i}", [128, 1024], BF16) for i in range(2)]
        MD = [sb(f"md{i}", [128, 1024], BF16) for i in range(2)]
        HCV = [sb(f"hcv{i}", [128, 512], BF16) for i in range(2)]
        T1W = [sb(f"t1w{i}", [128, 512], F32) for i in range(2)]
        T2W = [sb(f"t2w{i}", [128, 512], F32) for i in range(2)]
        T3W = [[sb(f"t3w{i}{p}", [128, 512], F32) for p in range(2)]
               for i in range(2)]
        T4W = [sb(f"t4w{i}", [128, 512], F32) for i in range(2)]
        YFN = [sb(f"yfn_b{i}", [128, INTER], BF16) for i in range(2)]
        JUNK = sb("junk", [128, 512], BF16)
        PSE1 = [ps.enter_context(nc.psum_tensor(f"pse1{i}", [128, 1024], F32))
                for i in range(2)]
        PSOF = [ps.enter_context(nc.psum_tensor(f"psof{i}", [128, 512], F32))
                for i in range(2)]
        PSDG = [ps.enter_context(nc.psum_tensor(f"psdg{i}", [128, 512], F32))
                for i in range(2)]

        def f_pre(q):
            ctx.dma(q, DWIDES[:], DWIDED, "d2")
            ctx.fence(q, "d2")
        blk.gpsimd(f_pre)

        ones_m = [0]

        def f_ones(v):
            ctx.W(v, "d2")
            for b in range(2):
                ctx.I(v.memset(LHSB[b][0:6, :], 1.0), "sv")
                ones_m[0] = ctx.I(v.memset(RHSB[b][0:6, :], 1.0), "sv")
        blk.vector(f_ones)

        hs_m = [[0, 0], [0, 0]]
        sg_m = [[0, 0], [0, 0]]
        hs_cons = [[(0, 0, 0), (0, 0, 0)], [(0, 0, 0), (0, 0, 0)]]
        sg_cons = [[0, 0], [0, 0]]
        esp_m = [0, 0]
        k6r_m = [0, 0]
        gm_m = [0]
        gmev_m = [0, 0]
        gmsb_cons = [0, 0]
        gmsb_cons_sv = [0, 0]
        l01c_m = [0, 0]
        l01c_cons = [0, 0]
        k6_m = [0, 0]
        pse1_cons = [0, 0]
        tw_m = [0, 0]
        tw_cons = [0, 0]
        exw_m = [0, 0]
        exw_cons = [0, 0]
        md_m = [0, 0]
        md_cons = [0, 0]
        diag_m = [0, 0]
        psdg_cons = [0, 0]
        hcv_m = [0]
        hcv_cons = [0]
        off_m = [0]
        psof_cons = [0, 0]
        t1_m = [0, 0]
        t1_cons = [0, 0]
        t2_m = [0, 0]
        t2_cons = [0, 0]
        t3_m = [[0, 0], [0, 0]]
        t4_m = [0, 0]
        yf_m = {}
        yst_m = [[0, 0], [0, 0]]

        def bc3(ap2d, reps, inner):
            return ap2d.unsqueeze(2).broadcast_to((128, reps, inner))

        def emit_t1sq(pc, pg, with_sq=True):
            for lt in range(2):
                US = U1S if lt == 1 else U0S

                def f(s, lt=lt, US=US, pc=pc, pg=pg):
                    ctx.W(s, "stn", off_m[0])
                    if t1_cons[lt]:
                        ctx.W(s, "sv", t1_cons[lt])
                    m = 0
                    for h in range(8):
                        hh = pg * 8 + h
                        m = ctx.I(s.activation(
                            T1W[lt][:, h * 64:(h + 1) * 64],
                            PSOF[lt][:, h * 64:(h + 1) * 64], AF.Identity,
                            scale=US[:, pc * 64 + hh: pc * 64 + hh + 1]), "ss")
                    t1_m[lt] = m
                    psof_cons[lt] = m
                return f

        sqj_m = [0]

        def emit_sq(s, qc, qg):
            for lt in range(2):
                ctx.W(s, "sg", yf_m[(qc, qg, lt)])
                if sqj_m[0]:
                    ctx.W(s, "ss", sqj_m[0])
                sqj_m[0] = ctx.I(s.activation(
                    JUNK[:], YFN[lt][:, qg * 512:(qg + 1) * 512], AF.Square,
                    accum_out=RSQ[:, (qc * 2 + lt) * 8 + qg:
                                  (qc * 2 + lt) * 8 + qg + 1]), "ss")


        def emit_lagged(pc, pg):
            pcp = pc % 2

            def f_t2(v, pc=pc, pg=pg):
                for lt in range(2):
                    ctx.W(v, "sv", t1_m[lt])
                    ctx.W(v, "stn", diag_m[lt])
                    if t2_cons[lt]:
                        ctx.W(v, "sg", t2_cons[lt])
                    t2_m[lt] = ctx.I(v.tensor_tensor(
                        T2W[lt][:], T1W[lt][:], PSDG[lt][:], OP.add), "sv")
                    t1_cons[lt] = t2_m[lt]
                    psdg_cons[lt] = t2_m[lt]
            blk.vector(f_t2)

            def f_t4(gp, pc=pc, pg=pg, pcp=pcp):
                pp = (pc * 8 + pg) % 2
                for lt in range(2):
                    ctx.W(gp, "sv", t2_m[lt])
                    ctx.W(gp, "sg", t3_m[lt][pp])
                    t4_m[lt] = ctx.I(gp.tensor_tensor(
                        T4W[lt][:], T2W[lt][:], T3W[lt][pp][:], OP.add), "sg")
                    t2_cons[lt] = t4_m[lt]
                    ctx.W(gp, "sg", t4_m[lt])
                    ctx.W(gp, "d0", sg_m[lt][pcp])
                    half = pg // 4
                    if yst_m[lt][half]:
                        ctx.W(gp, "d0", yst_m[lt][half])
                    yf_m[(pc, pg, lt)] = ctx.I(gp.tensor_tensor(
                        YFN[lt][:, pg * 512:(pg + 1) * 512], T4W[lt][:],
                        SILGT[lt][pcp][:, pg * 512:(pg + 1) * 512], OP.mult), "sg")
                    sg_cons[lt][pcp] = yf_m[(pc, pg, lt)]
            blk.gpsimd(f_t4)

            if pg == 3 or pg == 7:
                half = pg // 4

                def f_yst(q, pc=pc, half=half):
                    for lt in range(2):
                        ctx.W(q, "sg", yf_m[(pc, half * 4 + 3, lt)])
                        yst_m[lt][half] = ctx.dma(
                            q, yfn_scr[pc * CS + lt * 128: pc * CS + (lt + 1) * 128,
                                       half * 2048:(half + 1) * 2048],
                            YFN[lt][:, half * 2048:(half + 1) * 2048])
                    ctx.fence(q)
                    yst_m[0][half] = yst_m[1][half] = ctx.c["d0"]
                blk.sync(f_yst)

        steps = [(c, g) for c in range(NCH) for g in range(8)]
        for i, (c, g) in enumerate(steps):
            cp = c % 2
            gb = g % 2

            if g == 0:
                def f_ld(q, c=c, cp=cp):
                    for lt in range(2):
                        hc = hs_cons[lt][cp]
                        if hc[0]:
                            ctx.W(q, "stn", hc[0])
                        if hc[1]:
                            ctx.W(q, "sv", hc[1])
                        if hc[2]:
                            ctx.W(q, "sg", hc[2])
                        if sg_cons[lt][cp]:
                            ctx.W(q, "sg", sg_cons[lt][cp])
                        ctx.dmat(q, HS2[lt][cp][:],
                                 hid_scr[:, c * CS + lt * 128:
                                         c * CS + (lt + 1) * 128])
                        ctx.dma(q, SILGT[lt][cp][:],
                                silg_scr[c * CS + lt * 128:
                                         c * CS + (lt + 1) * 128, :])
                    ctx.fence(q)
                    hs_m[0][cp] = hs_m[1][cp] = ctx.c["d0"]
                    sg_m[0][cp] = sg_m[1][cp] = ctx.c["d0"]
                blk.sync(f_ld)

                def f_esp0(q, c=c):
                    if k6r_m[0]:
                        ctx.W(q, "stn", k6r_m[0])
                    else:
                        ctx.W(q, "sv", ones_m[0])
                    ctx.dma(q, LHSB[0][3:6, :],
                            ESP[c:c + 1, 3:6, 0:8, :].squeeze(0)
                            .rearrange("a b c -> a (b c)"))
                    ctx.dma(q, RHSB[0][0:3, :],
                            ESP[c:c + 1, 0:3, 0:8, :].squeeze(0)
                            .rearrange("a b c -> a (b c)"))
                    ctx.fence(q)
                    esp_m[0] = ctx.c["d0"]
                blk.sync(f_esp0)

                def f_gm(t, c=c):
                    if pse1_cons[0]:
                        ctx.W(t, "sv", pse1_cons[0])
                    for stt in range(2):
                        gm_m[0] = ctx.I(t.matmul(
                            PSE1[0][:, stt * 256:(stt + 1) * 256],
                            BCB[:, c * CS + stt * 128: c * CS + (stt + 1) * 128],
                            BCC[:, c * CS: (c + 1) * CS],
                            start=True, stop=True), "stn")
                blk.tensor(f_gm)

                def f_gmev(s, cp=cp):
                    ctx.W(s, "stn", gm_m[0])
                    if gmsb_cons[cp]:
                        ctx.W(s, "stn", gmsb_cons[cp])
                    if gmsb_cons_sv[cp]:
                        ctx.W(s, "sv", gmsb_cons_sv[cp])
                    gmev_m[cp] = ctx.I(s.copy(GMSB[cp][:], PSE1[0][:, 0:512]), "ss")
                blk.scalar(f_gmev)

            if g < 7:
                def f_esp(q, c=c, g=g):
                    ngb = (g + 1) % 2
                    if k6r_m[ngb]:
                        ctx.W(q, "stn", k6r_m[ngb])
                    else:
                        ctx.W(q, "sv", ones_m[0])
                    ctx.dma(q, LHSB[ngb][3:6, :],
                            ESP[c:c + 1, 3:6, (g + 1) * 8:(g + 2) * 8, :].squeeze(0)
                            .rearrange("a b c -> a (b c)"))
                    ctx.dma(q, RHSB[ngb][0:3, :],
                            ESP[c:c + 1, 0:3, (g + 1) * 8:(g + 2) * 8, :].squeeze(0)
                            .rearrange("a b c -> a (b c)"))
                    ctx.fence(q)
                    esp_m[ngb] = ctx.c["d0"]
                blk.sync(f_esp)

            for stt in range(2):
                def f_k6(t, g=g, gb=gb, stt=stt, cp=cp):
                    ctx.W(t, "d0", esp_m[gb])
                    if pse1_cons[stt]:
                        ctx.W(t, "sv", pse1_cons[stt])
                    if stt == 0:
                        ctx.W(t, "ss", gmev_m[cp])
                    m = 0
                    for h in range(8):
                        off = h * 256 + stt * 128
                        m = ctx.I(t.matmul(
                            PSE1[stt][:, h * 128:(h + 1) * 128],
                            LHSB[gb][0:6, off:off + 128],
                            RHSB[gb][0:6, off:off + 128],
                            start=True, stop=True), "stn")
                    k6_m[stt] = m
                    k6r_m[gb] = m
                blk.tensor(f_k6)

            def f_va(v, c=c, g=g, gb=gb):
                if l01c_cons[gb]:
                    ctx.W(v, "sv", l01c_cons[gb])
                l01c_m[gb] = ctx.I(v.tensor_tensor(
                    L01C[gb][:].rearrange("p (h j) -> p h j", j=128),
                    L01[:].unsqueeze(1).broadcast_to((128, 8, 128)),
                    bc3(CREPS[:, c * 64 + g * 8: c * 64 + g * 8 + 8], 8, 128),
                    OP.mult), "sv")
                ctx.W(v, "sv", l01c_m[gb])
                for stt in range(2):
                    ctx.W(v, "stn", k6_m[stt])
                    if tw_cons[stt]:
                        ctx.W(v, "ss", tw_cons[stt])
                    tw_m[stt] = ctx.I(v.tensor_tensor(
                        TWB[stt][:], PSE1[stt][:], L01C[gb][:], OP.add), "sv")
                    pse1_cons[stt] = tw_m[stt]
                    l01c_cons[gb] = tw_m[stt]
            blk.vector(f_va)

            def f_sa(s, i=i, c=c, g=g):
                for stt in range(2):
                    ctx.W(s, "sv", tw_m[stt])
                    if exw_cons[stt]:
                        ctx.W(s, "sv", exw_cons[stt])
                    exw_m[stt] = ctx.I(s.activation(EXW[stt][:], TWB[stt][:],
                                                    AF.Exp), "ss")
                    tw_cons[stt] = exw_m[stt]
                if i > 1:
                    qc, qg = steps[i - 2]
                    emit_sq(s, qc, qg)
            blk.scalar(f_sa)

            if i > 0:
                def f_t1(v, i=i):
                    pc, pg = steps[i - 1]
                    for lt in range(2):
                        US = U1S if lt == 1 else U0S
                        ctx.W(v, "stn", off_m[0])
                        if t1_cons[lt]:
                            ctx.W(v, "sv", t1_cons[lt])
                        t1_m[lt] = ctx.I(v.tensor_tensor(
                            T1W[lt][:].rearrange("p (h j) -> p h j", j=64),
                            PSOF[lt][:].rearrange("p (h j) -> p h j", j=64),
                            bc3(US[:, pc * 64 + pg * 8: pc * 64 + pg * 8 + 8],
                                8, 64), OP.mult), "sv")
                        psof_cons[lt] = t1_m[lt]
                blk.vector(f_t1)

            def f_vb(v, c=c, g=g, cp=cp):
                for stt in range(2):
                    ctx.W(v, "ss", exw_m[stt])
                    ctx.W(v, "ss", gmev_m[cp])
                    if md_cons[stt]:
                        ctx.W(v, "stn", md_cons[stt])
                    src = GMSB[cp][:, stt * 256 + stt * 128:
                                   stt * 256 + stt * 128 + 128]
                    md_m[stt] = ctx.I(v.tensor_tensor(
                        MD[stt][:].rearrange("p (h j) -> p h j", j=128),
                        EXW[stt][:].rearrange("p (h j) -> p h j", j=128),
                        src.unsqueeze(1).broadcast_to((128, 8, 128)),
                        OP.mult), "sv")
                    exw_cons[stt] = md_m[stt]
                    gmsb_cons_sv[cp] = md_m[stt]
                if hcv_cons[0]:
                    ctx.W(v, "stn", hcv_cons[0])
                ctx.W(v, "d0", max(hs_m[0][cp], hs_m[1][cp]))
                m = 0
                for var in range(2):
                    VD = V1D if var == 0 else V2D
                    m = ctx.I(v.tensor_tensor(
                        HCV[var][:].rearrange("p (h j) -> p h j", j=64),
                        HS2[var][cp][:, g * 512:(g + 1) * 512].rearrange(
                            "p (h j) -> p h j", j=64),
                        bc3(VD[:, c * 64 + g * 8: c * 64 + g * 8 + 8], 8, 64),
                        OP.mult), "sv")
                hcv_m[0] = m
                hs_cons[0][cp] = (hs_cons[0][cp][0], m, hs_cons[0][cp][2])
                hs_cons[1][cp] = (hs_cons[1][cp][0], m, hs_cons[1][cp][2])
            blk.vector(f_vb)

            def f_t3(gp, c=c, g=g, cp=cp):
                sp = (c * 8 + g) % 2
                for lt in range(2):
                    ctx.W(gp, "d0", hs_m[lt][cp])
                    ctx.W(gp, "d2")
                    t3_m[lt][sp] = ctx.I(gp.tensor_tensor(
                        T3W[lt][sp][:], HS2[lt][cp][:, g * 512:(g + 1) * 512],
                        DWIDES[:, g * 512:(g + 1) * 512], OP.mult), "sg")
                    hs_cons[lt][cp] = (hs_cons[lt][cp][0], hs_cons[lt][cp][1],
                                       t3_m[lt][sp])
            blk.gpsimd(f_t3)

            if i > 0:
                emit_lagged(*steps[i - 1])

            for stt in range(2):
                def f_diag(t, g=g, stt=stt, cp=cp):
                    ctx.W(t, "sv", md_m[stt])
                    ctx.W(t, "d0", hs_m[stt][cp])
                    if psdg_cons[stt]:
                        ctx.W(t, "sv", psdg_cons[stt])
                    m = 0
                    for h in range(8):
                        m = ctx.I(t.matmul(
                            PSDG[stt][:, h * 64:(h + 1) * 64],
                            MD[stt][:, h * 128:(h + 1) * 128],
                            HS2[stt][cp][:, (g * 8 + h) * 64:(g * 8 + h + 1) * 64],
                            start=True, stop=True), "stn")
                    diag_m[stt] = m
                    md_cons[stt] = m
                    hs_cons[stt][cp] = (m, hs_cons[stt][cp][1],
                                        hs_cons[stt][cp][2])
                blk.tensor(f_diag)

            def f_off(t, cp=cp):
                ctx.W(t, "sv", hcv_m[0])
                for lt in range(2):
                    if psof_cons[lt]:
                        ctx.W(t, "sv", psof_cons[lt])
                ctx.I(t.matmul(PSOF[1][:], GMSB[cp][:, 128:256], HCV[0][:],
                               start=True, stop=True), "stn")
                m = ctx.I(t.matmul(PSOF[0][:], GMSB[cp][:, 256:384], HCV[1][:],
                                   start=True, stop=True), "stn")
                off_m[0] = m
                hcv_cons[0] = m
                gmsb_cons[cp] = m
            blk.tensor(f_off)

        lc, lg = steps[-1]

        def f_flush1v(v):
            for lt in range(2):
                US = U1S if lt == 1 else U0S
                ctx.W(v, "stn", off_m[0])
                if t1_cons[lt]:
                    ctx.W(v, "sv", t1_cons[lt])
                t1_m[lt] = ctx.I(v.tensor_tensor(
                    T1W[lt][:].rearrange("p (h j) -> p h j", j=64),
                    PSOF[lt][:].rearrange("p (h j) -> p h j", j=64),
                    bc3(US[:, lc * 64 + lg * 8: lc * 64 + lg * 8 + 8], 8, 64),
                    OP.mult), "sv")
                psof_cons[lt] = t1_m[lt]
        blk.vector(f_flush1v)

        def f_flush1(s):
            qc, qg = steps[-2]
            emit_sq(s, qc, qg)
        blk.scalar(f_flush1)

        emit_lagged(lc, lg)

        def f_flush2(s):
            emit_sq(s, lc, lg)
        blk.scalar(f_flush2)

        def f_dr(q):
            ctx.W(q, "d0")
            ctx.W(q, "d1")
        blk.sync(f_dr)


def emit_p3(ctx, nc, T, TT, X, xn_scr, wout_scr, yfn_scr, q_scr, OUT,
            RSTD2, NW32D, RSQ, CC, ON1B, FBB):
    NHS = T // 256
    NGT = T // 128
    with nc.Block() as blk, ExitStack() as ps:
        sb = lambda n, sh, dt: ps.enter_context(nc.sbuf_tensor(n, sh, dt))
        WOTH = sb("woth", [128, 32 * 1024], BF16)
        QTH = sb("qth", [128, 16 * 1024], BF16)
        YFT = [sb(f"yft{i}", [128, 32 * 256], BF16) for i in range(2)]
        XNTS = [sb(f"xnts{i}", [128, 16 * 144], BF16) for i in range(4)]
        XRB = [sb(f"xrb{i}", [128, 1024], F32) for i in range(2)]
        SH = [sb(f"sh{i}", [128, 1024], F32) for i in range(2)]
        GTB = [sb(f"gtb{i}", [128, 512], F32) for i in range(4)]
        SGT = [sb(f"sgt{i}", [128, 512], F32) for i in range(2)]
        OTB = [sb(f"otb{i}", [128, 1024], F32) for i in range(2)]
        NW32 = sb("nw32s", [128, 32], F32)
        PSS = [ps.enter_context(nc.psum_tensor(f"pss{i}", [128, 512], F32))
               for i in range(2)]
        PSF = [ps.enter_context(nc.psum_tensor(f"psf{i}", [128, 512], F32))
               for i in range(4)]

        rs_m = [0]

        def f_rstd(v):
            ctx.W(v, "ss")
            m = 0
            for k in range(TT):
                m = ctx.I(v.tensor_reduce(RSTD2[:, k:k + 1],
                                          RSQ[:, k * 8:(k + 1) * 8], AX,
                                          OP.add), "sv")
            ctx.W(v, "sv", m)
        blk.vector(f_rstd)

        def f_rstd2(s):
            ctx.W(s, "sv")
            m = ctx.I(s.activation(RSTD2[:], RSTD2[:], AF.Ln,
                                   bias=CC[:, C_EPSRMS:C_EPSRMS + 1],
                                   scale=CC[:, C_1I:C_1I + 1]), "ss")
            ctx.W(s, "ss", m)
            rs_m[0] = ctx.I(s.activation(RSTD2[:], RSTD2[:], AF.Exp,
                                         scale=CC[:, C_NHALF:C_NHALF + 1]), "ss")
        blk.scalar(f_rstd2)

        def f_nwl(q):
            ctx.dma(q, NW32[:], NW32D)
            ctx.fence(q)
        blk.sync(f_nwl)

        qth_m = [0]
        woth_m = [0, 0]
        nws_m = [0, 0]
        yft_m = [0, 0]
        yft_cons = [0, 0]
        xnts_m = [0, 0, 0, 0]
        xnts_cons = [0, 0, 0, 0]
        xr_m = [0, 0]
        mm_s = [0, 0]
        mm_f = {}
        pss_ev = [0, 0]
        g_m = [0, 0, 0, 0]
        gt_cmb = [0, 0, 0, 0]
        sh_cmb = [0, 0]
        cmb_m = [0, 0]
        ot_st = [0, 0]
        first_ev = [True]

        def emit_xnl(gt, sl):
            def f_xnl(q, gt=gt, sl=sl):
                if xnts_cons[sl]:
                    ctx.W(q, "stn", xnts_cons[sl])
                xnts_m[sl] = ctx.dmat(
                    q, XNTS[sl][:].rearrange("p (m k) -> p m k", k=144),
                    xn_scr[gt * 128:gt * 128 + 144, :])
                ctx.fence(q)
            blk.sync(f_xnl)

        for eh in range(2):
            def f_qthl(q, eh=eh):
                if eh:
                    ctx.W(q, "stn")
                qth_m[0] = ctx.dmat(
                    q, QTH[:].rearrange("p (m k) -> p m k", k=1024),
                    q_scr[eh * 1024:(eh + 1) * 1024, :])
                ctx.fence(q)
            blk.sync(f_qthl)

            emit_xnl(0, 0)
            emit_xnl(1, 1)

            for kh in range(2):
                def f_wol(q, eh=eh, kh=kh):
                    if eh and kh == 0:
                        ctx.W(q, "stn")
                    woth_m[kh] = ctx.dmat(
                        q, WOTH[:, kh * 16 * 1024:(kh + 1) * 16 * 1024].rearrange(
                            "p (m k) -> p m k", k=1024),
                        wout_scr[eh * 1024:(eh + 1) * 1024,
                                 kh * 2048:(kh + 1) * 2048])
                    ctx.fence(q)
                blk.sync(f_wol)

                def f_nws(v, kh=kh):
                    ctx.W(v, "d0", woth_m[kh])
                    m = 0
                    for k in range(kh * 16, kh * 16 + 16):
                        m = ctx.I(v.tensor_scalar_mul(
                            WOTH[:, k * 1024:(k + 1) * 1024],
                            WOTH[:, k * 1024:(k + 1) * 1024],
                            NW32[:, k:k + 1]), "sv")
                    nws_m[kh] = m
                blk.vector(f_nws)

            for hs in range(NHS):
                b = hs % 2

                def f_yfl(q, b=b, hs=hs):
                    if yft_cons[b]:
                        ctx.W(q, "stn", yft_cons[b])
                    yft_m[b] = ctx.dmat(
                        q, YFT[b][:].rearrange("p (m k) -> p m k", k=256),
                        yfn_scr[hs * 256:(hs + 1) * 256, :])
                    ctx.fence(q)
                blk.sync(f_yfl)

                for q2 in range(2):
                    gt = hs * 2 + q2
                    g2 = gt % 2
                    sl = gt % 4
                    osem = "d4" if g2 == 0 else "d3"

                    def f_xrl(q, gt=gt, g2=g2, eh=eh):
                        if cmb_m[g2]:
                            ctx.W(q, "sv", cmb_m[g2])
                        xr_m[g2] = ctx.dma(
                            q, XRB[g2][:],
                            X[3 + gt * 128:131 + gt * 128,
                              eh * 1024:(eh + 1) * 1024], sem="d2")
                        ctx.fence(q, "d2")
                    blk.gpsimd(f_xrl)

                    def f_mmf(t, gt=gt, sl=sl, eh=eh):
                        ctx.W(t, "d0", qth_m[0])
                        ctx.W(t, "d0", xnts_m[sl])
                        for f in range(2):
                            fs = (gt * 2 + f) % 4
                            if g_m[fs]:
                                ctx.W(t, "ss", g_m[fs])
                            for it in range(16):
                                ctx.I(t.matmul(
                                    PSF[fs][:],
                                    XNTS[sl][:, it * 144 + 3:it * 144 + 131],
                                    QTH[:, it * 1024 + f * 512:
                                         it * 1024 + f * 512 + 512],
                                    start=(it == 0), stop=False), "stn")
                            mm_f[fs] = ctx.I(t.matmul(
                                PSF[fs][:], ON1B[0:1, 0:128],
                                FBB[0:1, eh * 1024 + f * 512:
                                     eh * 1024 + f * 512 + 512],
                                start=False, stop=True), "stn")
                        xnts_cons[sl] = mm_f[(gt * 2 + 1) % 4]
                    blk.tensor(f_mmf)

                    def f_mms(t, b=b, q2=q2):
                        ctx.W(t, "d0", yft_m[b])
                        for ew in range(2):
                            if pss_ev[ew]:
                                ctx.W(t, "ss", pss_ev[ew])
                            for kh in range(2):
                                if ew == 0:
                                    ctx.W(t, "sv", nws_m[kh])
                                for k in range(kh * 16, kh * 16 + 16):
                                    mm_s[ew] = ctx.I(t.matmul(
                                        PSS[ew][:],
                                        YFT[b][:, k * 256 + q2 * 128:
                                              k * 256 + q2 * 128 + 128],
                                        WOTH[:, k * 1024 + ew * 512:
                                             k * 1024 + ew * 512 + 512],
                                        start=(k == 0), stop=(k == 31)), "stn")
                        yft_cons[b] = mm_s[1]
                    blk.tensor(f_mms)

                    if gt + 2 < NGT:
                        emit_xnl(gt + 2, (gt + 2) % 4)

                    def f_evg(s, gt=gt, g2=g2):
                        if first_ev[0]:
                            ctx.W(s, "ss", rs_m[0])
                            first_ev[0] = False
                        for ew in range(2):
                            ctx.W(s, "stn", mm_s[ew])
                            if ew == 0 and sh_cmb[g2]:
                                ctx.W(s, "sv", sh_cmb[g2])
                            pss_ev[ew] = ctx.I(s.activation(
                                SH[g2][:, ew * 512:(ew + 1) * 512], PSS[ew][:],
                                AF.Identity, scale=RSTD2[:, gt:gt + 1]), "ss")
                        for f in range(2):
                            fs = (gt * 2 + f) % 4
                            ctx.W(s, "stn", mm_f[fs])
                            if gt_cmb[fs]:
                                ctx.W(s, "sv", gt_cmb[fs])
                            g_m[fs] = ctx.I(s.activation(GTB[fs][:], PSF[fs][:],
                                                         AF.Silu), "ss")
                    blk.scalar(f_evg)

                    def f_cmb(v, gt=gt, g2=g2, osem=osem):
                        ctx.W(v, "d2", xr_m[g2])
                        if ot_st[g2]:
                            ctx.W(v, osem, ot_st[g2])
                        for f in range(2):
                            fs = (gt * 2 + f) % 4
                            ctx.W(v, "ss", g_m[fs])
                            ctx.W(v, "ss", pss_ev[f])
                            m = ctx.I(v.tensor_tensor(
                                SGT[f][:], GTB[fs][:],
                                SH[g2][:, f * 512:(f + 1) * 512], OP.mult), "sv")
                            gt_cmb[fs] = m
                            if f == 1:
                                sh_cmb[g2] = m
                            ctx.W(v, "sv", m)
                            cmb_m[g2] = ctx.I(v.tensor_tensor(
                                OTB[g2][:, f * 512:(f + 1) * 512], SGT[f][:],
                                XRB[g2][:, f * 512:(f + 1) * 512], OP.add), "sv")
                    blk.vector(f_cmb)

                    def f_ost(q, gt=gt, g2=g2, eh=eh, osem=osem):
                        ctx.W(q, "sv", cmb_m[g2])
                        ot_st[g2] = ctx.dma(
                            q, OUT[gt * 128:(gt + 1) * 128,
                                   eh * 1024:(eh + 1) * 1024],
                            OTB[g2][:], sem=osem)
                        ctx.fence(q, osem)
                    if g2 == 0:
                        blk.scalar(f_ost)
                    else:
                        blk.gpsimd(f_ost)

        def f_dr(q):
            ctx.W(q, "d0")
        blk.sync(f_dr)


def _mk_inmaps(inputs, T=2048):
    x = np.ascontiguousarray(np.asarray(inputs["L0_attn_input_states"], np.float32))
    w_in = np.ascontiguousarray(np.asarray(inputs["L0_attn_W_in"], np.float32))
    conv_w = np.asarray(inputs["L0_attn_conv_w"], np.float32)
    conv_b = np.asarray(inputs["L0_attn_conv_b"], np.float32)
    dt_bias = np.asarray(inputs["L0_attn_dt_bias"], np.float32)
    a_log = np.asarray(inputs["L0_attn_A_log"], np.float32)
    dd = np.asarray(inputs["L0_attn_D"], np.float32)
    norm_w = np.asarray(inputs["L0_attn_norm_w"], np.float32)
    w_out = np.ascontiguousarray(np.asarray(inputs["L0_attn_W_out"], np.float32))
    ffn_w = np.ascontiguousarray(np.asarray(inputs["L0_ffn_weight"], np.float32))
    ffn_b = np.asarray(inputs["L0_ffn_bias"], np.float32)

    B, S, _ = x.shape
    npc = S // T
    conv_w_fm = np.ascontiguousarray(
        conv_w.reshape(NCONVT, 128, 4).transpose(1, 0, 2).reshape(128, NCONVT * 4))
    conv_b_fm = np.ascontiguousarray(conv_b.reshape(NCONVT, 128).T)
    nwrep = np.ascontiguousarray(np.broadcast_to(norm_w[None, :], (128, INTER)))
    nw32 = np.ascontiguousarray(norm_w.reshape(32, 128).T)
    dwide = np.ascontiguousarray(
        np.broadcast_to(np.repeat(dd, 64)[None, :], (128, INTER)))
    l01 = np.triu(np.ones((128, 128), np.float32)).astype(np.float32)
    identf = np.eye(128, dtype=np.float32)
    identb = np.eye(128, dtype=BF)
    ones1b = np.ones((1, 128), dtype=BF)
    ones4b = np.ones((4, 128), dtype=BF)
    shared = {
        "w_in": w_in, "w_out": w_out, "ffn_w": ffn_w,
        "ffn_b": np.ascontiguousarray(ffn_b.reshape(1, E)),
        "conv_w_fm": conv_w_fm, "conv_b_fm": conv_b_fm,
        "dtb": np.ascontiguousarray(dt_bias.reshape(64, 1)),
        "alog": np.ascontiguousarray(a_log.reshape(64, 1)),
        "nwrep": nwrep, "nw32": nw32, "dwide": dwide, "l01": l01,
        "identf": identf, "identb": identb,
        "ones1b": ones1b, "ones4b": ones4b,
    }
    in_maps = []
    for k in range(NCORES):
        b, t0 = k // npc, (k % npc) * T
        xs = np.zeros((T + 128, E), np.float32)
        if t0 >= 3:
            xs[0:3] = x[b, t0 - 3:t0]
        xs[3:3 + T] = x[b, t0:t0 + T]
        m = dict(shared)
        m["x"] = xs
        in_maps.append(m)
    return in_maps


def kernel(**inputs):
    T = 2048
    in_maps = _mk_inmaps(inputs, T)
    nc = build_nc(T)
    res = run_bass_kernel_spmd(nc, in_maps, core_ids=list(range(NCORES)))
    out = np.empty((2, 8192, E), np.float32)
    npc = 8192 // T
    for k in range(NCORES):
        b, t0 = k // npc, (k % npc) * T
        out[b, t0:t0 + T] = res.results[k]["out"]
    return out


# version 16
# speedup vs baseline: 1.1610x; 1.1610x over previous
import numpy as np
import ml_dtypes
from contextlib import ExitStack

import concourse.bass as bass
import concourse.mybir as mybir
from concourse.alu_op_type import AluOpType as OP
from concourse.bass_utils import run_bass_kernel_spmd

F32 = mybir.dt.float32
BF16 = mybir.dt.bfloat16
AF = mybir.ActivationFunctionType
AX = mybir.AxisListType.X
BF = ml_dtypes.bfloat16

E = 2048
PROJ = 8512
INTER = 4096
CS = 256
NCONVT = 34
EPS_LN = 1e-5
EPS_RMS = 1e-5
NCORES = 8

C_EPSLN, C_NHALF, C_07, C_1E, C_TINY, C_1I, C_EPSRMS, C_M1, C_ONE = range(9)
C_N = 12


class Ctx:
    def __init__(self, nc):
        self.nc = nc
        self.s = {}
        self.c = {}

    def mk(self, st, names):
        for n in names:
            self.s[n] = st.enter_context(self.nc.semaphore(n))
            self.c[n] = 0

    def I(self, inst, name, n=1):
        inst.then_inc(self.s[name], n)
        self.c[name] += n
        return self.c[name]

    def W(self, eng, name, mark=None):
        v = self.c[name] if mark is None else mark
        if v > 0:
            eng.wait_ge(self.s[name], v)

    def dma(self, q, dst, src, sem="d0"):
        return self.I(q.dma_start(dst, src), sem, 16)

    def fence(self, q, sem="d0"):
        q.wait_ge(self.s[sem], self.c[sem])

    def dmat(self, q, dst, src, sem="d0"):
        return self.I(q.dma_start_transpose(dst, src), sem, 16)


def build_nc(T=2048):
    assert T % 512 == 0
    NCH = T // CS
    TT = T // 128
    TPAD = T + 128
    NXT = TT + 1

    nc = bass.Bass("TRN2", target_bir_lowering=False)
    ctx = Ctx(nc)

    X = nc.dram_tensor("x", [TPAD, E], F32, kind="ExternalInput").ap()
    WIN = nc.dram_tensor("w_in", [PROJ, E], F32, kind="ExternalInput").ap()
    WOUT = nc.dram_tensor("w_out", [E, INTER], F32, kind="ExternalInput").ap()
    FFNW = nc.dram_tensor("ffn_w", [E, E], F32, kind="ExternalInput").ap()
    FFNB = nc.dram_tensor("ffn_b", [1, E], F32, kind="ExternalInput").ap()
    CWFM = nc.dram_tensor("conv_w_fm", [128, NCONVT * 4], F32, kind="ExternalInput").ap()
    CBFM = nc.dram_tensor("conv_b_fm", [128, NCONVT], F32, kind="ExternalInput").ap()
    DTBD = nc.dram_tensor("dtb", [64, 1], F32, kind="ExternalInput").ap()
    ALOGD = nc.dram_tensor("alog", [64, 1], F32, kind="ExternalInput").ap()
    NWREPD = nc.dram_tensor("nwrep", [128, INTER], F32, kind="ExternalInput").ap()
    NW32D = nc.dram_tensor("nw32", [128, 32], F32, kind="ExternalInput").ap()
    DWIDED = nc.dram_tensor("dwide", [128, INTER], F32, kind="ExternalInput").ap()
    L01D = nc.dram_tensor("l01", [128, 128], F32, kind="ExternalInput").ap()
    IDFD = nc.dram_tensor("identf", [128, 128], F32, kind="ExternalInput").ap()
    IDBD = nc.dram_tensor("identb", [128, 128], BF16, kind="ExternalInput").ap()
    ONES1D = nc.dram_tensor("ones1b", [1, 128], BF16, kind="ExternalInput").ap()
    ONES4D = nc.dram_tensor("ones4b", [4, 128], BF16, kind="ExternalInput").ap()

    OUT = nc.dram_tensor("out", [T, E], F32, kind="ExternalOutput").ap()

    xn_scr = nc.dram_tensor("xn_scr", [TPAD, E], BF16, kind="Internal").ap()
    win_scr = nc.dram_tensor("win_scr", [PROJ + 64, E], BF16, kind="Internal").ap()
    wout_scr = nc.dram_tensor("wout_scr", [E, INTER], BF16, kind="Internal").ap()
    q_scr = nc.dram_tensor("q_scr", [E, E], BF16, kind="Internal").ap()
    silg_scr = nc.dram_tensor("silg_scr", [T, INTER], BF16, kind="Internal").ap()
    hid_scr = nc.dram_tensor("hid_scr", [INTER, T], BF16, kind="Internal").ap()
    yfn_scr = nc.dram_tensor("yfn_scr", [T, INTER], BF16, kind="Internal").ap()
    ESP = nc.dram_tensor("esplit_scr", [NCH, 6, 64, CS], BF16, kind="Internal").ap()

    with ExitStack() as S0:
        ctx.mk(S0, ["d0", "d1", "d2", "d3", "d4", "d5", "sv", "ss", "stn", "sg"])
        sb0 = lambda n, sh, dt: S0.enter_context(nc.sbuf_tensor(n, sh, dt))

        CC = sb0("consts", [128, C_N], F32)
        CW = sb0("cwfm_s", [128, NCONVT * 4], F32)
        CB = sb0("cbfm_s", [128, NCONVT], F32)
        DTBS = sb0("dtbs", [64, 1], F32)
        ACOL = sb0("acol", [64, 1], F32)
        L01 = sb0("l01s", [128, 128], F32)
        IDFS = sb0("idfs", [128, 128], F32)
        IDBS = sb0("idbs", [128, 128], BF16)
        ON1B = sb0("on1b", [1, 128], BF16)
        ON4B = sb0("on4b", [4, 128], BF16)
        FBB = sb0("fbb", [1, E], BF16)
        MV = sb0("mv", [128, NXT * 2], F32)
        RSTD = sb0("rstd_ln", [128, NXT], F32)
        NMRS = sb0("nmrs", [128, NXT], F32)
        RSTD2 = sb0("rstd2", [128, TT], F32)
        RSQ = sb0("rsq_all", [128, NCH * 16], F32)

        with nc.named_scope("init"):
            emit_init(ctx, nc, CC, CW, CB, DTBS, ACOL, L01, IDFS, IDBS, ON1B, ON4B,
                      FBB, CWFM, CBFM, DTBD, ALOGD, L01D, IDFD, IDBD, ONES1D,
                      ONES4D, FFNB)
        HS = [(h * 512, 512) for h in range(16)] + [(8192, 384)]

        with ExitStack() as S2:
            sb2 = lambda n, sh, dt: S2.enter_context(nc.sbuf_tensor(n, sh, dt))
            BCB = sb2("bcb", [128, T], BF16)
            BCC = sb2("bcc", [128, T], BF16)
            V1D = sb2("v1d", [128, NCH * 64], F32)
            V2D = sb2("v2d", [128, NCH * 64], F32)
            U0S = sb2("u0s", [128, NCH * 64], F32)
            U1S = sb2("u1s", [128, NCH * 64], F32)
            CREPS = sb2("creps", [128, NCH * 64], F32)
            with ExitStack() as S1:
                DTRAW = S1.enter_context(nc.sbuf_tensor("dtraw", [64, TPAD], F32))
                XNT = S1.enter_context(nc.sbuf_tensor("xnt", [128, 16 * (T + 16)],
                                                      BF16))
                with nc.named_scope("p1a"):
                    wcast_m, xnt_ch = emit_p1a(ctx, nc, T, NXT, X, xn_scr, MV,
                                               RSTD, NMRS, CC, WIN, win_scr,
                                               XNT, HS)
                with nc.named_scope("p1b"):
                    emit_p1b(ctx, nc, T, TT, TPAD, xn_scr, win_scr, silg_scr,
                             hid_scr, DTRAW, CW, CB, BCB, BCC, CC, WOUT,
                             wout_scr, wcast_m, xnt_ch, XNT, HS, WIN, FFNW,
                             q_scr)
                with nc.named_scope("p15"):
                    emit_p15(ctx, nc, T, NCH, DTRAW, DTBS, ACOL, CC, IDFS, IDBS,
                             ON4B, ESP, V1D, V2D, U0S, U1S, CREPS)

            with nc.named_scope("p2"):
                emit_p2(ctx, nc, T, NCH, hid_scr, silg_scr, yfn_scr, ESP, RSQ,
                        DWIDED, BCB, BCC, L01, CREPS, V1D, V2D, U0S, U1S, CC)

        with nc.named_scope("p3"):
            emit_p3(ctx, nc, T, TT, X, xn_scr, wout_scr, yfn_scr, q_scr, OUT,
                    RSTD2, NW32D, RSQ, CC, ON1B, FBB)

        with nc.Block() as blk:
            def fin(q):
                ctx.W(q, "d0")
                ctx.W(q, "d1")
                ctx.W(q, "d2")
                ctx.W(q, "d3")
                ctx.W(q, "d4")
                ctx.W(q, "d5")
            blk.sync(fin)
    return nc


def emit_init(ctx, nc, CC, CW, CB, DTBS, ACOL, L01, IDFS, IDBS, ON1B, ON4B,
              FBB, CWFM, CBFM, DTBD, ALOGD, L01D, IDFD, IDBD, ONES1D, ONES4D,
              FFNB):
    with nc.Block() as blk:
        def f_loads(q):
            for dst, src in [(CW[:], CWFM), (CB[:], CBFM), (DTBS[:], DTBD),
                             (L01[:], L01D), (IDFS[:], IDFD), (IDBS[:], IDBD),
                             (ON1B[:], ONES1D), (ON4B[:], ONES4D),
                             (ACOL[:], ALOGD)]:
                ctx.dma(q, dst, src)
            ctx.fence(q)
        blk.sync(f_loads)

        def f_cast(g):
            ctx.dma(g, FBB[:], FFNB, "d1")
            ctx.fence(g, "d1")
        blk.gpsimd(f_cast)

        def f_sets(v):
            for i, val in enumerate([EPS_LN, -0.5, 0.7, 1.0 / E, 1e-18,
                                     1.0 / INTER, EPS_RMS, -1.0, 1.0]):
                ctx.I(v.memset(CC[:, i:i + 1], val), "sv")
        blk.vector(f_sets)

        def f_exp(s):
            ctx.W(s, "d0")
            ctx.I(s.activation(ACOL[:], ACOL[:], AF.Exp), "ss")
        blk.scalar(f_exp)

        def f_neg(v):
            ctx.W(v, "ss")
            ctx.I(v.tensor_scalar_mul(ACOL[:], ACOL[:], -1.0), "sv")
        blk.vector(f_neg)

        def f_dr(q):
            ctx.W(q, "d1")
        blk.sync(f_dr)


def emit_p1a(ctx, nc, T, NXT, X, xn_scr, MV, RSTD, NMRS, CC,
             WIN, win_scr, XNT, HS):
    XW = T + 16
    NG = (NXT + 3) // 4
    with nc.Block() as blk, ExitStack() as ps:
        sb = lambda n, sh, dt: ps.enter_context(nc.sbuf_tensor(n, sh, dt))
        XB4 = [sb(f"p1xb{i}", [128, E], F32) for i in range(4)]
        BNS = sb("p1bns", [128, 24], F32)
        XNB = [sb(f"p1xnb{i}", [128, E], BF16) for i in range(2)]

        ZPAD = sb("zpad", [64, E], BF16)
        zpad_m = [0]

        def f_zpad(v):
            zpad_m[0] = ctx.I(v.memset(ZPAD[:], 0.0), "sv")
        blk.vector(f_zpad)

        wcast_m = []

        def f_wprep(g):
            ctx.W(g, "sv", zpad_m[0])
            ctx.dma(g, win_scr[PROJ:PROJ + 64, :], ZPAD[:], "d1")
            ctx.fence(g, "d1")
            for jlo, wd in HS[:6]:
                m = ctx.dma(g, win_scr[jlo:jlo + wd, :],
                            WIN[jlo:jlo + wd, :], "d1")
                ctx.fence(g, "d1")
                wcast_m.append(m)
        blk.gpsimd(f_wprep)

        xnt_ch = []
        xl_m = [0] * NXT
        xl_sem = [""] * NXT
        xnact_m = [0, 0, 0, 0]
        xbread_m = [0] * NXT
        xnst_m = [0, 0]
        bn_m = [0] * NXT
        grp_rs = [0] * NG
        grp_nm = [0] * NG

        for gi in range(NG):
            lo = gi * 4
            hi = min(NXT, lo + 4)

            for xt in range(lo, hi):
                sl = xt % 4
                sem = "d0" if xt % 2 == 0 else "d4"

                def f_xl(q, xt=xt, sl=sl, sem=sem):
                    if xt >= 4:
                        ctx.W(q, "ss", xbread_m[xt - 4])
                    xl_m[xt] = ctx.dma(q, XB4[sl][:],
                                       X[xt * 128:(xt + 1) * 128, :], sem)
                    ctx.fence(q, sem)
                if xt % 2 == 0:
                    blk.sync(f_xl)
                else:
                    blk.scalar(f_xl)
                xl_sem[xt] = sem

            def f_bn(v, lo=lo, hi=hi):
                for xt in range(lo, hi):
                    ctx.W(v, xl_sem[xt], xl_m[xt])
                    m = 0
                    for k in range(4):
                        m = ctx.I(v.bn_stats(
                            BNS[:, k * 6:(k + 1) * 6],
                            XB4[xt % 4][:, k * 512:(k + 1) * 512]), "sv")
                    ctx.W(v, "sv", m)
                    bn_m[xt] = ctx.I(v.bn_aggr(MV[:, xt * 2:xt * 2 + 2],
                                               BNS[:]), "sv")
                    ctx.W(v, "sv", bn_m[xt])
            blk.vector(f_bn)

            def f_rs(s, lo=lo, hi=hi, gi=gi):
                ctx.W(s, "sv", bn_m[hi - 1])
                m = ctx.I(s.activation(RSTD[:, lo:hi], MV[:, lo * 2 + 1:hi * 2:2],
                                       AF.Ln, bias=CC[:, C_EPSLN:C_EPSLN + 1],
                                       scale=1.0), "ss")
                ctx.W(s, "ss", m)
                grp_rs[gi] = ctx.I(s.activation(RSTD[:, lo:hi], RSTD[:, lo:hi],
                                                AF.Exp,
                                                scale=CC[:, C_NHALF:C_NHALF + 1]),
                                   "ss")
            blk.scalar(f_rs)

            def f_nm(v, lo=lo, hi=hi, gi=gi):
                ctx.W(v, "ss", grp_rs[gi])
                m = ctx.I(v.tensor_tensor(NMRS[:, lo:hi], MV[:, lo * 2:hi * 2:2],
                                          RSTD[:, lo:hi], OP.mult), "sv")
                ctx.W(v, "sv", m)
                grp_nm[gi] = ctx.I(v.tensor_scalar_mul(NMRS[:, lo:hi],
                                                       NMRS[:, lo:hi], -1.0), "sv")
            blk.vector(f_nm)

            for xt in range(lo, hi):
                b = xt % 2

                def f_xn(s, b=b, xt=xt, gi=gi):
                    ctx.W(s, "sv", grp_nm[gi])
                    if xnst_m[b]:
                        ctx.W(s, "d0", xnst_m[b])
                    xbread_m[xt] = ctx.I(s.activation(
                        XNB[b][:], XB4[xt % 4][:], AF.Identity,
                        bias=NMRS[:, xt:xt + 1],
                        scale=RSTD[:, xt:xt + 1]), "ss")
                blk.scalar(f_xn)

                def f_xs(q, b=b, xt=xt):
                    ctx.W(q, "ss", xbread_m[xt])
                    xnst_m[b] = ctx.dma(q, xn_scr[xt * 128:(xt + 1) * 128, :],
                                        XNB[b][:])
                    ctx.fence(q)
                blk.sync(f_xs)

            done = min(hi * 128, XW)
            prev = len(xnt_ch) * 512
            while prev < done and done - prev >= 512 or (hi == NXT and prev < done):
                step = min(512, done - prev)

                def f_xnt(q, r0=prev, r1=prev + step):
                    m = ctx.dmat(
                        q, XNT[:].rearrange("p (m k) -> p m k", k=XW)[:, :, r0:r1],
                        xn_scr[r0:r1, :])
                    ctx.fence(q)
                    xnt_ch.append(m)
                blk.sync(f_xnt)
                prev += step

        def f_dr(q):
            ctx.W(q, "d0")
            ctx.W(q, "d4")
        blk.sync(f_dr)
        return wcast_m, xnt_ch


def emit_p1b(ctx, nc, T, TT, TPAD, xn_scr, win_scr, silg_scr, hid_scr,
             DTRAW, CW, CB, BCB, BCC, CC, WOUT, wout_scr, wcast_m, xnt_ch,
             XNT, HS, WIN, FFNW, q_scr):
    with nc.Block() as blk, ExitStack() as ps:
        sb = lambda n, sh, dt: ps.enter_context(nc.sbuf_tensor(n, sh, dt))
        FW = [sb(f"p0fw{i}", [128, E], F32) for i in range(1)]
        QA = sb("p0qa", [128, E], BF16)
        QBT = [sb(f"p0qb{i}", [128, E], BF16) for i in range(1)]
        TH = sb("p0th", [128, 4], F32)
        q_fl = [0, 0]
        q_dv = [0, 0]
        q_st = [0, 0]

        def f_wcast_rest(g):
            for jlo, wd in HS[6:]:
                wd = min(wd, PROJ - jlo)
                m = ctx.dma(g, win_scr[jlo:jlo + wd, :],
                            WIN[jlo:jlo + wd, :], "d1")
                ctx.fence(g, "d1")
                wcast_m.append(m)
        blk.gpsimd(f_wcast_rest)

        def emit_quant(ot):
            b = 0

            def f_fl(g, ot=ot, b=b):
                if q_dv[b]:
                    ctx.W(g, "sv", q_dv[b])
                q_fl[b] = ctx.dma(g, FW[b][:], FFNW[ot * 128:(ot + 1) * 128, :],
                                  "d2")
                ctx.fence(g, "d2")
            blk.gpsimd(f_fl)

            def f_qv(v, ot=ot, b=b):
                ctx.W(v, "d2", q_fl[b])
                ctx.W(v, "sv")
                m = ctx.I(v.tensor_reduce(TH[:, 0:1], FW[b][:], AX, OP.add,
                                          apply_absolute_value=True), "sv")
                ctx.W(v, "sv", m)
                m = ctx.I(v.tensor_scalar(TH[:, 1:2], TH[:, 0:1],
                                          CC[:, C_1E:C_1E + 1],
                                          CC[:, C_07:C_07 + 1], OP.mult,
                                          OP.mult), "sv")
                ctx.W(v, "sv", m)
                ctx.I(v.tensor_scalar(QA[:], FW[b][:], TH[:, 1:2], None,
                                      OP.is_gt), "sv")
                m = ctx.I(v.tensor_scalar(FW[b][:], FW[b][:], -1.0, TH[:, 1:2],
                                          OP.mult, OP.is_gt), "sv")
                ctx.W(v, "sv", m)
                q_dv[b] = ctx.I(v.tensor_tensor(QBT[b][:], QA[:], FW[b][:],
                                                OP.subtract), "sv")
            blk.vector(f_qv)

            def f_qs(g, ot=ot, b=b):
                ctx.W(g, "sv", q_dv[b])
                ctx.dma(g, q_scr[ot * 128:(ot + 1) * 128, :], QBT[b][:], "d2")
                ctx.fence(g, "d2")
            blk.gpsimd(f_qs)

        def f_woprep(g):
            ctx.dma(g, wout_scr, WOUT, "d2")
            ctx.fence(g, "d2")
        blk.gpsimd(f_woprep)

        XW = T + 16
        WTS = sb("wts", [128, 16 * 1024], BF16)
        SILGB = [sb(f"silgb{i}", [128, 512], BF16) for i in range(2)]
        CONVIN = [sb(f"convin{i}", [128, TPAD], F32) for i in range(2)]
        CVA = [sb(f"cva{i}", [128, T], F32) for i in range(2)]
        CVB = [sb(f"cvb{i}", [128, T], F32) for i in range(2)]
        HOUT = [sb(f"hout{i}", [128, T], BF16) for i in range(2)]
        PSG = [ps.enter_context(nc.psum_tensor(f"psg{i}", [128, 512], F32))
               for i in range(8)]

        def xnt_ap(it, lo, hi):
            return XNT[:, it * XW + lo: it * XW + hi]

        wts_m = [0, 0]
        wts_cons = [0, 0]

        def load_half(h):
            jlo, wd = HS[h]
            r = h % 2

            def f_wl(q, jlo=jlo, wd=wd, r=r, h=h):
                ctx.W(q, "d1", wcast_m[h])
                if wts_cons[r]:
                    ctx.W(q, "stn", wts_cons[r])
                dst = WTS[:].rearrange("p (m k) -> p m k", k=1024)[:, :, r * 512:r * 512 + wd]
                wts_m[r] = ctx.dmat(q, dst, win_scr[jlo:jlo + wd, :])
                ctx.fence(q)
            blk.sync(f_wl)

        load_half(0)

        psg_ev = [0] * 8
        silg_st = [0, 0]
        for h in range(8):
            r = h % 2
            load_half(h + 1)
            emit_quant(h)
            for tt0 in range(0, TT, 2):
                gmm = {}
                for tt in (tt0, tt0 + 1):
                    pb = (h * TT + tt) % 8

                    def f_mm(t, tt=tt, r=r, pb=pb):
                        ctx.W(t, "d0", wts_m[r])
                        ctx.W(t, "d0", xnt_ch[min((131 + tt * 128 - 1) // 512,
                                                  len(xnt_ch) - 1)])
                        if psg_ev[pb]:
                            ctx.W(t, "ss", psg_ev[pb])
                        for it in range(16):
                            wts_cons[r] = ctx.I(t.matmul(
                                PSG[pb][:],
                                xnt_ap(it, 3 + tt * 128, 131 + tt * 128),
                                WTS[:, it * 1024 + r * 512: it * 1024 + r * 512 + 512],
                                start=(it == 0), stop=(it == 15)), "stn")
                        gmm[tt] = wts_cons[r]
                    blk.tensor(f_mm)

                def f_ev(s, tt0=tt0, h=h, r=r, gmm=gmm):
                    for tt in (tt0, tt0 + 1):
                        ctx.W(s, "stn", gmm[tt])
                        pb = (h * TT + tt) % 8
                        sbuf_b = tt % 2
                        if silg_st[sbuf_b]:
                            ctx.W(s, "d0", silg_st[sbuf_b])
                        psg_ev[pb] = ctx.I(s.activation(
                            SILGB[sbuf_b][:], PSG[pb][:], AF.Silu), "ss")
                blk.scalar(f_ev)

                def f_sg(q, tt0=tt0, h=h):
                    for tt in (tt0, tt0 + 1):
                        pb = (h * TT + tt) % 8
                        sbuf_b = tt % 2
                        ctx.W(q, "ss", psg_ev[pb])
                        ctx.dma(
                            q, silg_scr[tt * 128:(tt + 1) * 128,
                                        h * 512:(h + 1) * 512],
                            SILGB[sbuf_b][:])
                    ctx.fence(q)
                    silg_st[0] = silg_st[1] = ctx.c["d0"]
                blk.sync(f_sg)

        TW = []
        off = 0
        while off < XW:
            TW.append((off, min(512, XW - off)))
            off += 512
        hst_m = [0, 0]
        conv_m = [0, 0]
        cin_cons = [0, 0]
        si_pend = [None]
        for h in range(8, 17):
            r = h % 2
            if h + 1 < 17:
                load_half(h + 1)
            if h - 8 < 8:
                emit_quant(8 + (h - 8))
            jlo, wd = HS[h]
            for jt in range(wd // 128):
                jglob = (jlo - 4096) // 128 + jt
                cb = jglob % 2
                ev_marks = [0]
                wmm = {}

                def emit_mmw(wi, lo, wdw, jt=jt, r=r):
                    def f_mmw(t, wi=wi, lo=lo, wdw=wdw, jt=jt, r=r):
                        ctx.W(t, "d0", wts_m[r])
                        ctx.W(t, "d0", xnt_ch[-1])
                        if psg_ev[wi % 8]:
                            ctx.W(t, "ss", psg_ev[wi % 8])
                        for it in range(16):
                            wts_cons[r] = ctx.I(t.matmul(
                                PSG[wi % 8][:, 0:wdw],
                                WTS[:, it * 1024 + r * 512 + jt * 128:
                                    it * 1024 + r * 512 + jt * 128 + 128],
                                xnt_ap(it, lo, lo + wdw),
                                start=(it == 0), stop=(it == 15)), "stn")
                        wmm[wi] = wts_cons[r]
                    blk.tensor(f_mmw)

                def emit_evw(wis, with_si, jglob=jglob, cb=cb, wmm=wmm):
                    def f_evw(s, wis=wis, with_si=with_si, jglob=jglob, cb=cb):
                        for wi in wis:
                            ctx.W(s, "stn", wmm[wi])
                            lo, wdw = TW[wi]
                            if wi == 0 and cin_cons[cb]:
                                ctx.W(s, "sv", cin_cons[cb])
                            if jglob < NCONVT:
                                psg_ev[wi % 8] = ctx.I(
                                    s.copy(CONVIN[cb][:, lo:lo + wdw],
                                           PSG[wi % 8][:, 0:wdw]), "ss")
                            else:
                                psg_ev[wi % 8] = ctx.I(
                                    s.copy(DTRAW[:, lo:lo + wdw],
                                           PSG[wi % 8][0:64, 0:wdw]), "ss")
                            ev_marks[0] = psg_ev[wi % 8]
                        if with_si and si_pend[0] is not None:
                            pj, pcb = si_pend[0]
                            ctx.W(s, "sv", cin_cons[pcb])
                            if pj < 32:
                                dst = HOUT[pcb][:]
                                if hst_m[pcb]:
                                    ctx.W(s, "d0", hst_m[pcb])
                            elif pj == 32:
                                dst = BCB[:]
                            else:
                                dst = BCC[:]
                            conv_m[pcb] = ctx.I(s.activation(
                                dst, CVB[pcb][:], AF.Silu,
                                bias=CB[:, pj:pj + 1], scale=1.0), "ss")
                            si_pend[0] = None
                    blk.scalar(f_evw)

                nw = len(TW)
                for wi in range(min(4, nw)):
                    emit_mmw(wi, TW[wi][0], TW[wi][1])
                emit_evw(list(range(min(4, nw))), with_si=True)
                if nw > 4:
                    for wi in range(4, nw):
                        emit_mmw(wi, TW[wi][0], TW[wi][1])
                    emit_evw(list(range(4, nw)), with_si=False)

                if jglob < NCONVT:
                    def f_cv(v, jglob=jglob, cb=cb):
                        ctx.W(v, "ss", ev_marks[0])
                        if conv_m[cb]:
                            ctx.W(v, "ss", conv_m[cb])
                        ctx.W(v, "sv")
                        m = ctx.I(v.tensor_scalar(
                            CVA[cb][:], CONVIN[cb][:, 0:T],
                            CW[:, jglob * 4:jglob * 4 + 1], None, OP.mult), "sv")
                        ctx.W(v, "sv", m)
                        m = ctx.I(v.scalar_tensor_tensor(
                            CVB[cb][:], CONVIN[cb][:, 1:1 + T],
                            CW[:, jglob * 4 + 1:jglob * 4 + 2], CVA[cb][:],
                            OP.mult, OP.add), "sv")
                        ctx.W(v, "sv", m)
                        m = ctx.I(v.scalar_tensor_tensor(
                            CVA[cb][:], CONVIN[cb][:, 2:2 + T],
                            CW[:, jglob * 4 + 2:jglob * 4 + 3], CVB[cb][:],
                            OP.mult, OP.add), "sv")
                        ctx.W(v, "sv", m)
                        cin_cons[cb] = ctx.I(v.scalar_tensor_tensor(
                            CVB[cb][:], CONVIN[cb][:, 3:3 + T],
                            CW[:, jglob * 4 + 3:jglob * 4 + 4], CVA[cb][:],
                            OP.mult, OP.add), "sv")
                    blk.vector(f_cv)
                    si_pend[0] = (jglob, cb)

                    if jglob >= 1 and (jglob - 1) < 32:
                        def f_hs(q, pj=jglob - 1):
                            pcb = pj % 2
                            ctx.W(q, "ss", conv_m[pcb])
                            hst_m[pcb] = ctx.dma(
                                q, hid_scr[pj * 128:(pj + 1) * 128, :],
                                HOUT[pcb][:])
                            ctx.fence(q)
                        blk.sync(f_hs)

        def f_sil(s):
            if si_pend[0] is not None:
                pj, pcb = si_pend[0]
                ctx.W(s, "sv", cin_cons[pcb])
                dst = BCC[:] if pj == 33 else (BCB[:] if pj == 32 else HOUT[pcb][:])
                conv_m[pcb] = ctx.I(s.activation(
                    dst, CVB[pcb][:], AF.Silu,
                    bias=CB[:, pj:pj + 1], scale=1.0), "ss")
                si_pend[0] = None
        blk.scalar(f_sil)

        def f_dr(q):
            ctx.W(q, "d0")
            ctx.W(q, "d1")
        blk.sync(f_dr)


def emit_p15(ctx, nc, T, NCH, DTRAW, DTBS, ACOL, CC, IDFS, IDBS, ON4B,
             ESP, V1D, V2D, U0S, U1S, CREPS):
    with nc.Block() as blk, ExitStack() as ps:
        sb = lambda n, sh, dt: ps.enter_context(nc.sbuf_tensor(n, sh, dt))
        SP1 = sb("sp1", [64, T], F32)
        MX = sb("mx_t", [64, T], F32)
        DT = sb("dt_t", [64, T], F32)
        ACUM = sb("acum", [64, T], F32)
        LNDT = sb("lndt", [64, T], F32)
        NN = sb("nn_t", [64, T], F32)
        NEG = sb("neg_t", [64, T], F32)
        SC2 = sb("sc2", [64, CS], F32)
        SPL = sb("spl", [64, T], BF16)
        RF = sb("rf_t", [64, T], F32)
        RES = sb("res_t", [64, T], F32)
        CHS = sb("chs", [64, NCH * 8], F32)
        VTMP = sb("vtmp", [64, NCH * 512], F32)
        CST = sb("cst", [64, NCH * 4], BF16)
        CR4 = [sb(f"cr4_{i}", [4, 64], BF16) for i in range(2)]
        PTF = [ps.enter_context(nc.psum_tensor(f"ptf{i}", [128, 512], F32))
               for i in range(2)]
        PTB = [ps.enter_context(nc.psum_tensor(f"ptb{i}", [4, 64], BF16))
               for i in range(2)]

        def f_v1(v):
            ctx.W(v, "ss")
            ctx.I(v.tensor_scalar(MX[:], DTRAW[:, 3:3 + T], DTBS[:], 0.0,
                                  OP.add, OP.max), "sv")
        blk.vector(f_v1)

        def f_s1(s):
            ctx.W(s, "ss")
            m = ctx.I(s.activation(SP1[:], DTRAW[:, 3:3 + T], AF.Abs,
                                   bias=DTBS[:], scale=1.0), "ss")
            ctx.W(s, "ss", m)
            m = ctx.I(s.activation(SP1[:], SP1[:], AF.Exp,
                                   scale=CC[0:64, C_M1:C_M1 + 1]), "ss")
            ctx.W(s, "ss", m)
            ctx.I(s.activation(SP1[:], SP1[:], AF.Ln, bias=1.0), "ss")
        blk.scalar(f_s1)

        def f_v2(v):
            ctx.W(v, "ss")
            ctx.W(v, "sv")
            m = ctx.I(v.tensor_tensor(DT[:], MX[:], SP1[:], OP.add), "sv")
            ctx.W(v, "sv", m)
            m = ctx.I(v.tensor_scalar(MX[:], DT[:], ACOL[:], None, OP.mult), "sv")
            ctx.W(v, "sv", m)
            ctx.I(v.tensor_scalar(SP1[:], DT[:], CC[0:64, C_TINY:C_TINY + 1],
                                  None, OP.max), "sv")
        blk.vector(f_v2)

        def f_s2(s):
            ctx.W(s, "sv")
            ctx.I(s.activation(LNDT[:], SP1[:], AF.Ln), "ss")
        blk.scalar(f_s2)

        SC2A = sb("sc2a", [64, T], F32)

        def f_cum(v):
            ctx.W(v, "sv")
            curs = [MX[:, c * CS:(c + 1) * CS] for c in range(NCH)]
            for step in range(8):
                k = 1 << step
                m = 0
                for c in range(NCH):
                    lo = c * CS
                    dst = (SC2A[:, lo:lo + CS] if step % 2 == 0
                           else ACUM[:, lo:lo + CS])
                    cur = curs[c]
                    ctx.I(v.tensor_copy(dst[:, 0:k], cur[:, 0:k]), "sv")
                    m = ctx.I(v.tensor_tensor(dst[:, k:CS], cur[:, k:CS],
                                              cur[:, 0:CS - k], OP.add), "sv")
                    curs[c] = dst
                ctx.W(v, "sv", m)
        blk.vector(f_cum)

        def f_v3(v):
            ctx.W(v, "sv")
            ctx.W(v, "ss")
            m = ctx.I(v.scalar_tensor_tensor(NN[:], ACUM[:], -1.0, LNDT[:],
                                             OP.mult, OP.add), "sv")
            ctx.W(v, "sv", m)
            for c in range(NCH):
                lo = c * CS
                ctx.I(v.tensor_scalar(NEG[:, lo:lo + CS], NN[:, lo:lo + CS],
                                      ACUM[:, lo + CS - 1:lo + CS], None,
                                      OP.add), "sv")
        blk.vector(f_v3)

        def emit_split(src, base_row):
            spl_m = [0]

            def f_sa(v, src=src):
                ctx.W(v, "sv")
                ctx.W(v, "d0")
                spl_m[0] = ctx.I(v.tensor_copy(SPL[:], src[:]), "sv")
            blk.vector(f_sa)

            def f_wa(q, base_row=base_row):
                ctx.W(q, "sv", spl_m[0])
                for c in range(NCH):
                    ctx.dma(q, ESP[c:c + 1, base_row:base_row + 1, :, :].squeeze(0).squeeze(0),
                            SPL[:, c * CS:(c + 1) * CS])
                ctx.fence(q)
            blk.sync(f_wa)

            def f_sbf(v, src=src):
                ctx.W(v, "sv")
                m = ctx.I(v.tensor_copy(RF[:], SPL[:]), "sv")
                ctx.W(v, "sv", m)
                m = ctx.I(v.tensor_tensor(RES[:], src[:], RF[:], OP.subtract), "sv")
                ctx.W(v, "sv", m)
                ctx.W(v, "d0")
                spl_m[0] = ctx.I(v.tensor_copy(SPL[:], RES[:]), "sv")
            blk.vector(f_sbf)

            def f_wb(q, base_row=base_row):
                ctx.W(q, "sv", spl_m[0])
                for c in range(NCH):
                    ctx.dma(q, ESP[c:c + 1, base_row + 1:base_row + 2, :, :].squeeze(0).squeeze(0),
                            SPL[:, c * CS:(c + 1) * CS])
                ctx.fence(q)
            blk.sync(f_wb)

            def f_sc(v):
                ctx.W(v, "sv")
                m = ctx.I(v.tensor_copy(RF[:], SPL[:]), "sv")
                ctx.W(v, "sv", m)
                m = ctx.I(v.tensor_tensor(RES[:], RES[:], RF[:], OP.subtract), "sv")
                ctx.W(v, "sv", m)
                ctx.W(v, "d0")
                spl_m[0] = ctx.I(v.tensor_copy(SPL[:], RES[:]), "sv")
            blk.vector(f_sc)

            def f_wc(q, base_row=base_row):
                ctx.W(q, "sv", spl_m[0])
                for c in range(NCH):
                    ctx.dma(q, ESP[c:c + 1, base_row + 2:base_row + 3, :, :].squeeze(0).squeeze(0),
                            SPL[:, c * CS:(c + 1) * CS])
                ctx.fence(q)
            blk.sync(f_wc)

        emit_split(ACUM, 0)
        emit_split(NEG, 3)

        def f_cs(s):
            ctx.W(s, "sv")
            m = 0
            for c in range(NCH):
                lo = c * CS
                m = ctx.I(s.activation(CHS[:, c * 8:c * 8 + 1],
                                       ACUM[:, lo + CS - 1:lo + CS], AF.Exp), "ss")
            ctx.W(s, "ss", m)
            for c in range(NCH):
                m = ctx.I(s.activation(CHS[:, c * 8 + 1:c * 8 + 2],
                                       CHS[:, c * 8:c * 8 + 1], AF.Ln,
                                       bias=1.0), "ss")
        blk.scalar(f_cs)

        def f_cv2(v):
            ctx.W(v, "ss")
            ctx.W(v, "sv")
            for c in range(NCH):
                lo = c * CS
                o8 = c * 8
                A255 = ACUM[:, lo + CS - 1:lo + CS]
                P127 = ACUM[:, lo + 127:lo + 128]
                Q0 = ACUM[:, lo:lo + 1]
                sp = CHS[:, o8 + 1:o8 + 2]
                ctx.I(v.tensor_tensor(CHS[:, o8 + 2:o8 + 3], sp, A255,
                                      OP.subtract), "sv")
                ctx.I(v.tensor_tensor(CHS[:, o8 + 3:o8 + 4], sp, P127,
                                      OP.subtract), "sv")
                ctx.I(v.tensor_tensor(CHS[:, o8 + 4:o8 + 5], Q0, A255,
                                      OP.add), "sv")
                ctx.I(v.tensor_scalar_mul(CHS[:, o8 + 5:o8 + 6], Q0, -1.0), "sv")
        blk.vector(f_cv2)

        def f_uv(s):
            ctx.W(s, "sv")
            ctx.W(s, "ss")
            for c in range(NCH):
                lo = c * CS
                o8 = c * 8
                ctx.I(s.activation(VTMP[:, c * 512:c * 512 + 128],
                                   NN[:, lo:lo + 128], AF.Exp,
                                   bias=ACUM[:, lo + 127:lo + 128], scale=1.0), "ss")
                ctx.I(s.activation(VTMP[:, c * 512 + 128:c * 512 + 256],
                                   NN[:, lo + 128:lo + 256], AF.Exp,
                                   bias=CHS[:, o8 + 4:o8 + 5], scale=1.0), "ss")
                ctx.I(s.activation(VTMP[:, c * 512 + 256:c * 512 + 384],
                                   ACUM[:, lo + 128:lo + 256], AF.Exp,
                                   bias=CHS[:, o8 + 3:o8 + 4], scale=1.0), "ss")
                ctx.I(s.activation(VTMP[:, c * 512 + 384:c * 512 + 512],
                                   ACUM[:, lo:lo + 128], AF.Exp,
                                   bias=CHS[:, o8 + 5:o8 + 6], scale=1.0), "ss")
        blk.scalar(f_uv)

        def f_cspl(v):
            ctx.W(v, "sv")
            for lvl in range(4):
                m = 0
                for c in range(NCH):
                    o8 = c * 8
                    src = CHS[:, o8 + 2:o8 + 3] if lvl == 0 else CHS[:, o8 + 6:o8 + 7]
                    m = ctx.I(v.tensor_copy(CST[:, c * 4 + lvl:c * 4 + lvl + 1],
                                            src), "sv")
                ctx.W(v, "sv", m)
                if lvl < 3:
                    for c in range(NCH):
                        o8 = c * 8
                        m = ctx.I(v.tensor_copy(
                            CHS[:, o8 + 7:o8 + 8],
                            CST[:, c * 4 + lvl:c * 4 + lvl + 1]), "sv")
                    ctx.W(v, "sv", m)
                    for c in range(NCH):
                        o8 = c * 8
                        src = CHS[:, o8 + 2:o8 + 3] if lvl == 0 else CHS[:, o8 + 6:o8 + 7]
                        m = ctx.I(v.tensor_tensor(CHS[:, o8 + 6:o8 + 7], src,
                                                  CHS[:, o8 + 7:o8 + 8],
                                                  OP.subtract), "sv")
                    ctx.W(v, "sv", m)
        blk.vector(f_cspl)

        tp_m = [0, 0]
        tev_m = [0, 0]
        ctp_m = [0, 0]
        crv_m = [0, 0]
        crm_m = [0, 0]
        crev_m = [0, 0]
        for c in range(NCH):
            pp = c % 2

            def f_tp(t, c=c, pp=pp):
                if c == 0:
                    ctx.W(t, "ss")
                if tev_m[pp]:
                    ctx.W(t, "sv", tev_m[pp])
                m = 0
                for k in range(4):
                    m = ctx.I(t.transpose(PTF[pp][:, k * 64:(k + 1) * 64],
                                          VTMP[:, c * 512 + k * 128:
                                               c * 512 + (k + 1) * 128],
                                          IDFS[0:64, 0:64]), "stn")
                tp_m[pp] = m
                if c == 0:
                    ctx.W(t, "sv")
                if crv_m[pp]:
                    ctx.W(t, "sv", crv_m[pp])
                ctp_m[pp] = ctx.I(t.transpose(
                    PTB[pp][0:4, 0:64], CST[:, c * 4:(c + 1) * 4],
                    IDBS[0:64, 0:64]), "stn")
            blk.tensor(f_tp)

            def f_tev(v, c=c, pp=pp):
                ctx.W(v, "stn", tp_m[pp])
                m = 0
                for k, dstt in enumerate([V1D, V2D, U1S, U0S]):
                    m = ctx.I(v.tensor_copy(dstt[:, c * 64:(c + 1) * 64],
                                            PTF[pp][:, k * 64:(k + 1) * 64]), "sv")
                tev_m[pp] = m
                ctx.W(v, "stn", ctp_m[pp])
                crv_m[pp] = ctx.I(v.tensor_copy(CR4[pp][:], PTB[pp][0:4, 0:64]),
                                  "sv")
            blk.vector(f_tev)

            def f_crm(t, c=c, pp=pp):
                ctx.W(t, "sv", crv_m[pp])
                if crev_m[pp]:
                    ctx.W(t, "sv", crev_m[pp])
                crm_m[pp] = ctx.I(t.matmul(PTF[pp][:, 256:320], ON4B[:],
                                           CR4[pp][:], start=True, stop=True),
                                  "stn")
            blk.tensor(f_crm)

            def f_crev(v, c=c, pp=pp):
                ctx.W(v, "stn", crm_m[pp])
                crev_m[pp] = ctx.I(v.tensor_copy(
                    CREPS[:, c * 64:(c + 1) * 64], PTF[pp][:, 256:320]), "sv")
            blk.vector(f_crev)

        def f_dr(q):
            ctx.W(q, "d0")
        blk.sync(f_dr)


def emit_p2(ctx, nc, T, NCH, hid_scr, silg_scr, yfn_scr, ESP, RSQ, DWIDED,
            BCB, BCC, L01, CREPS, V1D, V2D, U0S, U1S, CC):
    with nc.Block() as blk, ExitStack() as ps:
        sb = lambda n, sh, dt: ps.enter_context(nc.sbuf_tensor(n, sh, dt))
        HS2 = [[sb(f"hs2_{lt}{p}", [128, INTER], BF16) for p in range(2)]
               for lt in range(2)]
        SILGT = [[sb(f"silgt{lt}{p}", [128, INTER], BF16) for p in range(2)]
                 for lt in range(2)]
        DWIDES = sb("dwides", [128, INTER], BF16)
        GMSB = [sb(f"gmsb{p}", [128, 512], BF16) for p in range(2)]
        LHSB = [sb(f"lhsb{i}", [6, 2048], BF16) for i in range(2)]
        RHSB = [sb(f"rhsb{i}", [6, 2048], BF16) for i in range(2)]
        L01C = [sb(f"l01c{i}", [128, 1024], F32) for i in range(2)]
        TWB = [sb(f"twb{i}", [128, 1024], F32) for i in range(2)]
        EXW = [sb(f"exw{i}", [128, 1024], BF16) for i in range(2)]
        MD = [sb(f"md{i}", [128, 1024], BF16) for i in range(2)]
        HCV = [sb(f"hcv{i}", [128, 512], BF16) for i in range(2)]
        T1W = [sb(f"t1w{i}", [128, 512], F32) for i in range(2)]
        T2W = [sb(f"t2w{i}", [128, 512], F32) for i in range(2)]
        T3W = [[sb(f"t3w{i}{p}", [128, 512], F32) for p in range(2)]
               for i in range(2)]
        T4W = [sb(f"t4w{i}", [128, 512], F32) for i in range(2)]
        YFN = [sb(f"yfn_b{i}", [128, INTER], BF16) for i in range(2)]
        JUNK = sb("junk", [128, 512], BF16)
        PSE1 = [ps.enter_context(nc.psum_tensor(f"pse1{i}", [128, 1024], F32))
                for i in range(2)]
        PSOF = [ps.enter_context(nc.psum_tensor(f"psof{i}", [128, 512], F32))
                for i in range(2)]
        PSDG = [ps.enter_context(nc.psum_tensor(f"psdg{i}", [128, 512], F32))
                for i in range(2)]

        def f_pre(q):
            ctx.dma(q, DWIDES[:], DWIDED, "d2")
            ctx.fence(q, "d2")
        blk.gpsimd(f_pre)

        ones_m = [0]

        def f_ones(v):
            ctx.W(v, "d2")
            for b in range(2):
                ctx.I(v.memset(LHSB[b][0:6, :], 1.0), "sv")
                ones_m[0] = ctx.I(v.memset(RHSB[b][0:6, :], 1.0), "sv")
        blk.vector(f_ones)

        hs_m = [[0, 0], [0, 0]]
        sg_m = [[0, 0], [0, 0]]
        hs_cons = [[(0, 0, 0), (0, 0, 0)], [(0, 0, 0), (0, 0, 0)]]
        sg_cons = [[0, 0], [0, 0]]
        esp_m = [0, 0]
        k6r_m = [0, 0]
        gm_m = [0]
        gmev_m = [0, 0]
        gmsb_cons = [0, 0]
        gmsb_cons_sv = [0, 0]
        l01c_m = [0, 0]
        l01c_cons = [0, 0]
        k6_m = [0, 0]
        pse1_cons = [0, 0]
        tw_m = [0, 0]
        tw_cons = [0, 0]
        exw_m = [0, 0]
        exw_cons = [0, 0]
        md_m = [0, 0]
        md_cons = [0, 0]
        diag_m = [0, 0]
        psdg_cons = [0, 0]
        hcv_m = [0]
        hcv_cons = [0]
        off_m = [0]
        psof_cons = [0, 0]
        t1_m = [0, 0]
        t1_cons = [0, 0]
        t2_m = [0, 0]
        t2_cons = [0, 0]
        t3_m = [[0, 0], [0, 0]]
        t4_m = [0, 0]
        yf_m = {}
        yst_m = [[0, 0], [0, 0]]

        def bc3(ap2d, reps, inner):
            return ap2d.unsqueeze(2).broadcast_to((128, reps, inner))

        def emit_t1sq(pc, pg, with_sq=True):
            for lt in range(2):
                US = U1S if lt == 1 else U0S

                def f(s, lt=lt, US=US, pc=pc, pg=pg):
                    ctx.W(s, "stn", off_m[0])
                    if t1_cons[lt]:
                        ctx.W(s, "sv", t1_cons[lt])
                    m = 0
                    for h in range(8):
                        hh = pg * 8 + h
                        m = ctx.I(s.activation(
                            T1W[lt][:, h * 64:(h + 1) * 64],
                            PSOF[lt][:, h * 64:(h + 1) * 64], AF.Identity,
                            scale=US[:, pc * 64 + hh: pc * 64 + hh + 1]), "ss")
                    t1_m[lt] = m
                    psof_cons[lt] = m
                return f

        sqj_m = [0]

        def emit_sq(s, qc, qg):
            for lt in range(2):
                ctx.W(s, "sg", yf_m[(qc, qg, lt)])
                if sqj_m[0]:
                    ctx.W(s, "ss", sqj_m[0])
                sqj_m[0] = ctx.I(s.activation(
                    JUNK[:], YFN[lt][:, qg * 512:(qg + 1) * 512], AF.Square,
                    accum_out=RSQ[:, (qc * 2 + lt) * 8 + qg:
                                  (qc * 2 + lt) * 8 + qg + 1]), "ss")


        def emit_lagged(pc, pg):
            pcp = pc % 2

            def f_t2(v, pc=pc, pg=pg):
                for lt in range(2):
                    ctx.W(v, "sv", t1_m[lt])
                    ctx.W(v, "stn", diag_m[lt])
                    if t2_cons[lt]:
                        ctx.W(v, "sg", t2_cons[lt])
                    t2_m[lt] = ctx.I(v.tensor_tensor(
                        T2W[lt][:], T1W[lt][:], PSDG[lt][:], OP.add), "sv")
                    t1_cons[lt] = t2_m[lt]
                    psdg_cons[lt] = t2_m[lt]
            blk.vector(f_t2)

            def f_t4(gp, pc=pc, pg=pg, pcp=pcp):
                pp = (pc * 8 + pg) % 2
                for lt in range(2):
                    ctx.W(gp, "sv", t2_m[lt])
                    ctx.W(gp, "sg", t3_m[lt][pp])
                    t4_m[lt] = ctx.I(gp.tensor_tensor(
                        T4W[lt][:], T2W[lt][:], T3W[lt][pp][:], OP.add), "sg")
                    t2_cons[lt] = t4_m[lt]
                    ctx.W(gp, "sg", t4_m[lt])
                    ctx.W(gp, "d0", sg_m[lt][pcp])
                    half = pg // 4
                    if yst_m[lt][half]:
                        ctx.W(gp, "d0", yst_m[lt][half])
                    yf_m[(pc, pg, lt)] = ctx.I(gp.tensor_tensor(
                        YFN[lt][:, pg * 512:(pg + 1) * 512], T4W[lt][:],
                        SILGT[lt][pcp][:, pg * 512:(pg + 1) * 512], OP.mult), "sg")
                    sg_cons[lt][pcp] = yf_m[(pc, pg, lt)]
            blk.gpsimd(f_t4)

            if pg == 3 or pg == 7:
                half = pg // 4

                def f_yst(q, pc=pc, half=half):
                    for lt in range(2):
                        ctx.W(q, "sg", yf_m[(pc, half * 4 + 3, lt)])
                        yst_m[lt][half] = ctx.dma(
                            q, yfn_scr[pc * CS + lt * 128: pc * CS + (lt + 1) * 128,
                                       half * 2048:(half + 1) * 2048],
                            YFN[lt][:, half * 2048:(half + 1) * 2048])
                    ctx.fence(q)
                    yst_m[0][half] = yst_m[1][half] = ctx.c["d0"]
                blk.sync(f_yst)

        steps = [(c, g) for c in range(NCH) for g in range(8)]
        for i, (c, g) in enumerate(steps):
            cp = c % 2
            gb = g % 2

            if g == 0:
                def f_ld(q, c=c, cp=cp):
                    for lt in range(2):
                        hc = hs_cons[lt][cp]
                        if hc[0]:
                            ctx.W(q, "stn", hc[0])
                        if hc[1]:
                            ctx.W(q, "sv", hc[1])
                        if hc[2]:
                            ctx.W(q, "sg", hc[2])
                        if sg_cons[lt][cp]:
                            ctx.W(q, "sg", sg_cons[lt][cp])
                        ctx.dmat(q, HS2[lt][cp][:],
                                 hid_scr[:, c * CS + lt * 128:
                                         c * CS + (lt + 1) * 128])
                        ctx.dma(q, SILGT[lt][cp][:],
                                silg_scr[c * CS + lt * 128:
                                         c * CS + (lt + 1) * 128, :])
                    ctx.fence(q)
                    hs_m[0][cp] = hs_m[1][cp] = ctx.c["d0"]
                    sg_m[0][cp] = sg_m[1][cp] = ctx.c["d0"]
                blk.sync(f_ld)

                def f_esp0(q, c=c):
                    if k6r_m[0]:
                        ctx.W(q, "stn", k6r_m[0])
                    else:
                        ctx.W(q, "sv", ones_m[0])
                    ctx.dma(q, LHSB[0][3:6, :],
                            ESP[c:c + 1, 3:6, 0:8, :].squeeze(0)
                            .rearrange("a b c -> a (b c)"))
                    ctx.dma(q, RHSB[0][0:3, :],
                            ESP[c:c + 1, 0:3, 0:8, :].squeeze(0)
                            .rearrange("a b c -> a (b c)"))
                    ctx.fence(q)
                    esp_m[0] = ctx.c["d0"]
                blk.sync(f_esp0)

                def f_gm(t, c=c):
                    if pse1_cons[0]:
                        ctx.W(t, "sv", pse1_cons[0])
                    for stt in range(2):
                        gm_m[0] = ctx.I(t.matmul(
                            PSE1[0][:, stt * 256:(stt + 1) * 256],
                            BCB[:, c * CS + stt * 128: c * CS + (stt + 1) * 128],
                            BCC[:, c * CS: (c + 1) * CS],
                            start=True, stop=True), "stn")
                blk.tensor(f_gm)

                def f_gmev(s, cp=cp):
                    ctx.W(s, "stn", gm_m[0])
                    if gmsb_cons[cp]:
                        ctx.W(s, "stn", gmsb_cons[cp])
                    if gmsb_cons_sv[cp]:
                        ctx.W(s, "sv", gmsb_cons_sv[cp])
                    gmev_m[cp] = ctx.I(s.copy(GMSB[cp][:], PSE1[0][:, 0:512]), "ss")
                blk.scalar(f_gmev)

            if g < 7:
                def f_esp(q, c=c, g=g):
                    ngb = (g + 1) % 2
                    if k6r_m[ngb]:
                        ctx.W(q, "stn", k6r_m[ngb])
                    else:
                        ctx.W(q, "sv", ones_m[0])
                    ctx.dma(q, LHSB[ngb][3:6, :],
                            ESP[c:c + 1, 3:6, (g + 1) * 8:(g + 2) * 8, :].squeeze(0)
                            .rearrange("a b c -> a (b c)"))
                    ctx.dma(q, RHSB[ngb][0:3, :],
                            ESP[c:c + 1, 0:3, (g + 1) * 8:(g + 2) * 8, :].squeeze(0)
                            .rearrange("a b c -> a (b c)"))
                    ctx.fence(q)
                    esp_m[ngb] = ctx.c["d0"]
                blk.sync(f_esp)

            for stt in range(2):
                def f_k6(t, g=g, gb=gb, stt=stt, cp=cp):
                    ctx.W(t, "d0", esp_m[gb])
                    if pse1_cons[stt]:
                        ctx.W(t, "sv", pse1_cons[stt])
                    if stt == 0:
                        ctx.W(t, "ss", gmev_m[cp])
                    m = 0
                    for h in range(8):
                        off = h * 256 + stt * 128
                        m = ctx.I(t.matmul(
                            PSE1[stt][:, h * 128:(h + 1) * 128],
                            LHSB[gb][0:6, off:off + 128],
                            RHSB[gb][0:6, off:off + 128],
                            start=True, stop=True), "stn")
                    k6_m[stt] = m
                    k6r_m[gb] = m
                blk.tensor(f_k6)

            def f_va(v, c=c, g=g, gb=gb):
                if l01c_cons[gb]:
                    ctx.W(v, "sv", l01c_cons[gb])
                l01c_m[gb] = ctx.I(v.tensor_tensor(
                    L01C[gb][:].rearrange("p (h j) -> p h j", j=128),
                    L01[:].unsqueeze(1).broadcast_to((128, 8, 128)),
                    bc3(CREPS[:, c * 64 + g * 8: c * 64 + g * 8 + 8], 8, 128),
                    OP.mult), "sv")
                ctx.W(v, "sv", l01c_m[gb])
                for stt in range(2):
                    ctx.W(v, "stn", k6_m[stt])
                    if tw_cons[stt]:
                        ctx.W(v, "ss", tw_cons[stt])
                    tw_m[stt] = ctx.I(v.tensor_tensor(
                        TWB[stt][:], PSE1[stt][:], L01C[gb][:], OP.add), "sv")
                    pse1_cons[stt] = tw_m[stt]
                    l01c_cons[gb] = tw_m[stt]
            blk.vector(f_va)

            def f_sa(s, i=i, c=c, g=g):
                for stt in range(2):
                    ctx.W(s, "sv", tw_m[stt])
                    if exw_cons[stt]:
                        ctx.W(s, "sv", exw_cons[stt])
                    exw_m[stt] = ctx.I(s.activation(EXW[stt][:], TWB[stt][:],
                                                    AF.Exp), "ss")
                    tw_cons[stt] = exw_m[stt]
                if i > 1:
                    qc, qg = steps[i - 2]
                    emit_sq(s, qc, qg)
            blk.scalar(f_sa)

            if i > 0:
                def f_t1(v, i=i):
                    pc, pg = steps[i - 1]
                    for lt in range(2):
                        US = U1S if lt == 1 else U0S
                        ctx.W(v, "stn", off_m[0])
                        if t1_cons[lt]:
                            ctx.W(v, "sv", t1_cons[lt])
                        t1_m[lt] = ctx.I(v.tensor_tensor(
                            T1W[lt][:].rearrange("p (h j) -> p h j", j=64),
                            PSOF[lt][:].rearrange("p (h j) -> p h j", j=64),
                            bc3(US[:, pc * 64 + pg * 8: pc * 64 + pg * 8 + 8],
                                8, 64), OP.mult), "sv")
                        psof_cons[lt] = t1_m[lt]
                blk.vector(f_t1)

            def f_vb(v, c=c, g=g, cp=cp):
                for stt in range(2):
                    ctx.W(v, "ss", exw_m[stt])
                    ctx.W(v, "ss", gmev_m[cp])
                    if md_cons[stt]:
                        ctx.W(v, "stn", md_cons[stt])
                    src = GMSB[cp][:, stt * 256 + stt * 128:
                                   stt * 256 + stt * 128 + 128]
                    md_m[stt] = ctx.I(v.tensor_tensor(
                        MD[stt][:].rearrange("p (h j) -> p h j", j=128),
                        EXW[stt][:].rearrange("p (h j) -> p h j", j=128),
                        src.unsqueeze(1).broadcast_to((128, 8, 128)),
                        OP.mult), "sv")
                    exw_cons[stt] = md_m[stt]
                    gmsb_cons_sv[cp] = md_m[stt]
                if hcv_cons[0]:
                    ctx.W(v, "stn", hcv_cons[0])
                ctx.W(v, "d0", max(hs_m[0][cp], hs_m[1][cp]))
                m = 0
                for var in range(2):
                    VD = V1D if var == 0 else V2D
                    m = ctx.I(v.tensor_tensor(
                        HCV[var][:].rearrange("p (h j) -> p h j", j=64),
                        HS2[var][cp][:, g * 512:(g + 1) * 512].rearrange(
                            "p (h j) -> p h j", j=64),
                        bc3(VD[:, c * 64 + g * 8: c * 64 + g * 8 + 8], 8, 64),
                        OP.mult), "sv")
                hcv_m[0] = m
                hs_cons[0][cp] = (hs_cons[0][cp][0], m, hs_cons[0][cp][2])
                hs_cons[1][cp] = (hs_cons[1][cp][0], m, hs_cons[1][cp][2])
            blk.vector(f_vb)

            def f_t3(gp, c=c, g=g, cp=cp):
                sp = (c * 8 + g) % 2
                for lt in range(2):
                    ctx.W(gp, "d0", hs_m[lt][cp])
                    ctx.W(gp, "d2")
                    t3_m[lt][sp] = ctx.I(gp.tensor_tensor(
                        T3W[lt][sp][:], HS2[lt][cp][:, g * 512:(g + 1) * 512],
                        DWIDES[:, g * 512:(g + 1) * 512], OP.mult), "sg")
                    hs_cons[lt][cp] = (hs_cons[lt][cp][0], hs_cons[lt][cp][1],
                                       t3_m[lt][sp])
            blk.gpsimd(f_t3)

            if i > 0:
                emit_lagged(*steps[i - 1])

            for stt in range(2):
                def f_diag(t, g=g, stt=stt, cp=cp):
                    ctx.W(t, "sv", md_m[stt])
                    ctx.W(t, "d0", hs_m[stt][cp])
                    if psdg_cons[stt]:
                        ctx.W(t, "sv", psdg_cons[stt])
                    m = 0
                    for h in range(8):
                        m = ctx.I(t.matmul(
                            PSDG[stt][:, h * 64:(h + 1) * 64],
                            MD[stt][:, h * 128:(h + 1) * 128],
                            HS2[stt][cp][:, (g * 8 + h) * 64:(g * 8 + h + 1) * 64],
                            start=True, stop=True), "stn")
                    diag_m[stt] = m
                    md_cons[stt] = m
                    hs_cons[stt][cp] = (m, hs_cons[stt][cp][1],
                                        hs_cons[stt][cp][2])
                blk.tensor(f_diag)

            def f_off(t, cp=cp):
                ctx.W(t, "sv", hcv_m[0])
                for lt in range(2):
                    if psof_cons[lt]:
                        ctx.W(t, "sv", psof_cons[lt])
                ctx.I(t.matmul(PSOF[1][:], GMSB[cp][:, 128:256], HCV[0][:],
                               start=True, stop=True), "stn")
                m = ctx.I(t.matmul(PSOF[0][:], GMSB[cp][:, 256:384], HCV[1][:],
                                   start=True, stop=True), "stn")
                off_m[0] = m
                hcv_cons[0] = m
                gmsb_cons[cp] = m
            blk.tensor(f_off)

        lc, lg = steps[-1]

        def f_flush1v(v):
            for lt in range(2):
                US = U1S if lt == 1 else U0S
                ctx.W(v, "stn", off_m[0])
                if t1_cons[lt]:
                    ctx.W(v, "sv", t1_cons[lt])
                t1_m[lt] = ctx.I(v.tensor_tensor(
                    T1W[lt][:].rearrange("p (h j) -> p h j", j=64),
                    PSOF[lt][:].rearrange("p (h j) -> p h j", j=64),
                    bc3(US[:, lc * 64 + lg * 8: lc * 64 + lg * 8 + 8], 8, 64),
                    OP.mult), "sv")
                psof_cons[lt] = t1_m[lt]
        blk.vector(f_flush1v)

        def f_flush1(s):
            qc, qg = steps[-2]
            emit_sq(s, qc, qg)
        blk.scalar(f_flush1)

        emit_lagged(lc, lg)

        def f_flush2(s):
            emit_sq(s, lc, lg)
        blk.scalar(f_flush2)

        def f_dr(q):
            ctx.W(q, "d0")
            ctx.W(q, "d1")
        blk.sync(f_dr)


def emit_p3(ctx, nc, T, TT, X, xn_scr, wout_scr, yfn_scr, q_scr, OUT,
            RSTD2, NW32D, RSQ, CC, ON1B, FBB):
    NHS = T // 256
    NGT = T // 128
    with nc.Block() as blk, ExitStack() as ps:
        sb = lambda n, sh, dt: ps.enter_context(nc.sbuf_tensor(n, sh, dt))
        WOTH = sb("woth", [128, 32 * 1024], BF16)
        QTH = sb("qth", [128, 16 * 1024], BF16)
        YFT = [sb(f"yft{i}", [128, 32 * 256], BF16) for i in range(2)]
        XNTS = [sb(f"xnts{i}", [128, 16 * 144], BF16) for i in range(4)]
        XRB = [sb(f"xrb{i}", [128, 1024], F32) for i in range(2)]
        SH = [sb(f"sh{i}", [128, 1024], F32) for i in range(2)]
        GTB = [sb(f"gtb{i}", [128, 512], F32) for i in range(4)]
        SGT = [sb(f"sgt{i}", [128, 512], F32) for i in range(2)]
        OTB = [sb(f"otb{i}", [128, 1024], F32) for i in range(2)]
        NW32 = sb("nw32s", [128, 32], F32)
        PSS = [ps.enter_context(nc.psum_tensor(f"pss{i}", [128, 512], F32))
               for i in range(2)]
        PSF = [ps.enter_context(nc.psum_tensor(f"psf{i}", [128, 512], F32))
               for i in range(4)]

        rs_m = [0]

        def f_rstd(v):
            ctx.W(v, "ss")
            m = 0
            for k in range(TT):
                m = ctx.I(v.tensor_reduce(RSTD2[:, k:k + 1],
                                          RSQ[:, k * 8:(k + 1) * 8], AX,
                                          OP.add), "sv")
            ctx.W(v, "sv", m)
        blk.vector(f_rstd)

        def f_rstd2(s):
            ctx.W(s, "sv")
            m = ctx.I(s.activation(RSTD2[:], RSTD2[:], AF.Ln,
                                   bias=CC[:, C_EPSRMS:C_EPSRMS + 1],
                                   scale=CC[:, C_1I:C_1I + 1]), "ss")
            ctx.W(s, "ss", m)
            rs_m[0] = ctx.I(s.activation(RSTD2[:], RSTD2[:], AF.Exp,
                                         scale=CC[:, C_NHALF:C_NHALF + 1]), "ss")
        blk.scalar(f_rstd2)

        def f_nwl(q):
            ctx.dma(q, NW32[:], NW32D)
            ctx.fence(q)
        blk.sync(f_nwl)

        qth_m = [0]
        woth_m = [0, 0]
        nws_m = [0, 0]
        yft_m = [0, 0]
        yft_cons = [0, 0]
        xnts_m = [0, 0, 0, 0]
        xnts_cons = [0, 0, 0, 0]
        xr_m = [0, 0]
        mm_s = [0, 0]
        mm_f = {}
        pss_ev = [0, 0]
        g_m = [0, 0, 0, 0]
        gt_cmb = [0, 0, 0, 0]
        sh_cmb = [0, 0]
        cmb_m = [0, 0]
        ot_st = [0, 0]
        first_ev = [True]

        def emit_xnl(gt, sl):
            def f_xnl(q, gt=gt, sl=sl):
                if xnts_cons[sl]:
                    ctx.W(q, "stn", xnts_cons[sl])
                xnts_m[sl] = ctx.dmat(
                    q, XNTS[sl][:].rearrange("p (m k) -> p m k", k=144),
                    xn_scr[gt * 128:gt * 128 + 144, :])
                ctx.fence(q)
            blk.sync(f_xnl)

        for eh in range(2):
            def f_qthl(q, eh=eh):
                if eh:
                    ctx.W(q, "stn")
                qth_m[0] = ctx.dmat(
                    q, QTH[:].rearrange("p (m k) -> p m k", k=1024),
                    q_scr[eh * 1024:(eh + 1) * 1024, :])
                ctx.fence(q)
            blk.sync(f_qthl)

            emit_xnl(0, 0)
            emit_xnl(1, 1)

            for kh in range(2):
                def f_wol(q, eh=eh, kh=kh):
                    if eh and kh == 0:
                        ctx.W(q, "stn")
                    woth_m[kh] = ctx.dmat(
                        q, WOTH[:, kh * 16 * 1024:(kh + 1) * 16 * 1024].rearrange(
                            "p (m k) -> p m k", k=1024),
                        wout_scr[eh * 1024:(eh + 1) * 1024,
                                 kh * 2048:(kh + 1) * 2048])
                    ctx.fence(q)
                blk.sync(f_wol)

                def f_nws(v, kh=kh):
                    ctx.W(v, "d0", woth_m[kh])
                    m = 0
                    for k in range(kh * 16, kh * 16 + 16):
                        m = ctx.I(v.tensor_scalar_mul(
                            WOTH[:, k * 1024:(k + 1) * 1024],
                            WOTH[:, k * 1024:(k + 1) * 1024],
                            NW32[:, k:k + 1]), "sv")
                    nws_m[kh] = m
                blk.vector(f_nws)

            for hs in range(NHS):
                b = hs % 2

                def f_yfl(q, b=b, hs=hs):
                    if yft_cons[b]:
                        ctx.W(q, "stn", yft_cons[b])
                    yft_m[b] = ctx.dmat(
                        q, YFT[b][:].rearrange("p (m k) -> p m k", k=256),
                        yfn_scr[hs * 256:(hs + 1) * 256, :])
                    ctx.fence(q)
                blk.sync(f_yfl)

                for q2 in range(2):
                    gt = hs * 2 + q2
                    g2 = gt % 2
                    sl = gt % 4
                    osem = "d4" if g2 == 0 else "d3"

                    def f_xrl(q, gt=gt, g2=g2, eh=eh):
                        if cmb_m[g2]:
                            ctx.W(q, "sv", cmb_m[g2])
                        xr_m[g2] = ctx.dma(
                            q, XRB[g2][:],
                            X[3 + gt * 128:131 + gt * 128,
                              eh * 1024:(eh + 1) * 1024], sem="d2")
                        ctx.fence(q, "d2")
                    blk.gpsimd(f_xrl)

                    def f_mmf(t, gt=gt, sl=sl, eh=eh):
                        ctx.W(t, "d0", qth_m[0])
                        ctx.W(t, "d0", xnts_m[sl])
                        for f in range(2):
                            fs = (gt * 2 + f) % 4
                            if g_m[fs]:
                                ctx.W(t, "ss", g_m[fs])
                            for it in range(16):
                                ctx.I(t.matmul(
                                    PSF[fs][:],
                                    XNTS[sl][:, it * 144 + 3:it * 144 + 131],
                                    QTH[:, it * 1024 + f * 512:
                                         it * 1024 + f * 512 + 512],
                                    start=(it == 0), stop=False), "stn")
                            mm_f[fs] = ctx.I(t.matmul(
                                PSF[fs][:], ON1B[0:1, 0:128],
                                FBB[0:1, eh * 1024 + f * 512:
                                     eh * 1024 + f * 512 + 512],
                                start=False, stop=True), "stn")
                        xnts_cons[sl] = mm_f[(gt * 2 + 1) % 4]
                    blk.tensor(f_mmf)

                    def f_mms(t, b=b, q2=q2):
                        ctx.W(t, "d0", yft_m[b])
                        for ew in range(2):
                            if pss_ev[ew]:
                                ctx.W(t, "ss", pss_ev[ew])
                            for kh in range(2):
                                if ew == 0:
                                    ctx.W(t, "sv", nws_m[kh])
                                for k in range(kh * 16, kh * 16 + 16):
                                    mm_s[ew] = ctx.I(t.matmul(
                                        PSS[ew][:],
                                        YFT[b][:, k * 256 + q2 * 128:
                                              k * 256 + q2 * 128 + 128],
                                        WOTH[:, k * 1024 + ew * 512:
                                             k * 1024 + ew * 512 + 512],
                                        start=(k == 0), stop=(k == 31)), "stn")
                        yft_cons[b] = mm_s[1]
                    blk.tensor(f_mms)

                    if gt + 2 < NGT:
                        emit_xnl(gt + 2, (gt + 2) % 4)

                    def f_evg(s, gt=gt, g2=g2):
                        if first_ev[0]:
                            ctx.W(s, "ss", rs_m[0])
                            first_ev[0] = False
                        for ew in range(2):
                            ctx.W(s, "stn", mm_s[ew])
                            if ew == 0 and sh_cmb[g2]:
                                ctx.W(s, "sv", sh_cmb[g2])
                            pss_ev[ew] = ctx.I(s.activation(
                                SH[g2][:, ew * 512:(ew + 1) * 512], PSS[ew][:],
                                AF.Identity, scale=RSTD2[:, gt:gt + 1]), "ss")
                        for f in range(2):
                            fs = (gt * 2 + f) % 4
                            ctx.W(s, "stn", mm_f[fs])
                            if gt_cmb[fs]:
                                ctx.W(s, "sv", gt_cmb[fs])
                            g_m[fs] = ctx.I(s.activation(GTB[fs][:], PSF[fs][:],
                                                         AF.Silu), "ss")
                    blk.scalar(f_evg)

                    def f_cmb(v, gt=gt, g2=g2, osem=osem):
                        ctx.W(v, "d2", xr_m[g2])
                        if ot_st[g2]:
                            ctx.W(v, osem, ot_st[g2])
                        for f in range(2):
                            fs = (gt * 2 + f) % 4
                            ctx.W(v, "ss", g_m[fs])
                            ctx.W(v, "ss", pss_ev[f])
                            m = ctx.I(v.tensor_tensor(
                                SGT[f][:], GTB[fs][:],
                                SH[g2][:, f * 512:(f + 1) * 512], OP.mult), "sv")
                            gt_cmb[fs] = m
                            if f == 1:
                                sh_cmb[g2] = m
                            ctx.W(v, "sv", m)
                            cmb_m[g2] = ctx.I(v.tensor_tensor(
                                OTB[g2][:, f * 512:(f + 1) * 512], SGT[f][:],
                                XRB[g2][:, f * 512:(f + 1) * 512], OP.add), "sv")
                    blk.vector(f_cmb)

                    def f_ost(q, gt=gt, g2=g2, eh=eh, osem=osem):
                        ctx.W(q, "sv", cmb_m[g2])
                        ot_st[g2] = ctx.dma(
                            q, OUT[gt * 128:(gt + 1) * 128,
                                   eh * 1024:(eh + 1) * 1024],
                            OTB[g2][:], sem=osem)
                        ctx.fence(q, osem)
                    if g2 == 0:
                        blk.scalar(f_ost)
                    else:
                        blk.gpsimd(f_ost)

        def f_dr(q):
            ctx.W(q, "d0")
        blk.sync(f_dr)


def _mk_inmaps(inputs, T=2048):
    x = np.ascontiguousarray(np.asarray(inputs["L0_attn_input_states"], np.float32))
    w_in = np.ascontiguousarray(np.asarray(inputs["L0_attn_W_in"], np.float32))
    conv_w = np.asarray(inputs["L0_attn_conv_w"], np.float32)
    conv_b = np.asarray(inputs["L0_attn_conv_b"], np.float32)
    dt_bias = np.asarray(inputs["L0_attn_dt_bias"], np.float32)
    a_log = np.asarray(inputs["L0_attn_A_log"], np.float32)
    dd = np.asarray(inputs["L0_attn_D"], np.float32)
    norm_w = np.asarray(inputs["L0_attn_norm_w"], np.float32)
    w_out = np.ascontiguousarray(np.asarray(inputs["L0_attn_W_out"], np.float32))
    ffn_w = np.ascontiguousarray(np.asarray(inputs["L0_ffn_weight"], np.float32))
    ffn_b = np.asarray(inputs["L0_ffn_bias"], np.float32)

    B, S, _ = x.shape
    npc = S // T
    conv_w_fm = np.ascontiguousarray(
        conv_w.reshape(NCONVT, 128, 4).transpose(1, 0, 2).reshape(128, NCONVT * 4))
    conv_b_fm = np.ascontiguousarray(conv_b.reshape(NCONVT, 128).T)
    nwrep = np.ascontiguousarray(np.broadcast_to(norm_w[None, :], (128, INTER)))
    nw32 = np.ascontiguousarray(norm_w.reshape(32, 128).T)
    dwide = np.ascontiguousarray(
        np.broadcast_to(np.repeat(dd, 64)[None, :], (128, INTER)))
    l01 = np.triu(np.ones((128, 128), np.float32)).astype(np.float32)
    identf = np.eye(128, dtype=np.float32)
    identb = np.eye(128, dtype=BF)
    ones1b = np.ones((1, 128), dtype=BF)
    ones4b = np.ones((4, 128), dtype=BF)
    shared = {
        "w_in": w_in, "w_out": w_out, "ffn_w": ffn_w,
        "ffn_b": np.ascontiguousarray(ffn_b.reshape(1, E)),
        "conv_w_fm": conv_w_fm, "conv_b_fm": conv_b_fm,
        "dtb": np.ascontiguousarray(dt_bias.reshape(64, 1)),
        "alog": np.ascontiguousarray(a_log.reshape(64, 1)),
        "nwrep": nwrep, "nw32": nw32, "dwide": dwide, "l01": l01,
        "identf": identf, "identb": identb,
        "ones1b": ones1b, "ones4b": ones4b,
    }
    in_maps = []
    for k in range(NCORES):
        b, t0 = k // npc, (k % npc) * T
        xs = np.zeros((T + 128, E), np.float32)
        if t0 >= 3:
            xs[0:3] = x[b, t0 - 3:t0]
        xs[3:3 + T] = x[b, t0:t0 + T]
        m = dict(shared)
        m["x"] = xs
        in_maps.append(m)
    return in_maps


def kernel(**inputs):
    T = 2048
    in_maps = _mk_inmaps(inputs, T)
    nc = build_nc(T)
    res = run_bass_kernel_spmd(nc, in_maps, core_ids=list(range(NCORES)))
    out = np.empty((2, 8192, E), np.float32)
    npc = 8192 // T
    for k in range(NCORES):
        b, t0 = k // npc, (k % npc) * T
        out[b, t0:t0 + T] = res.results[k]["out"]
    return out
